# version 02
# baseline (speedup 1.0000x reference)
import numpy as np
import ml_dtypes
import concourse.bass as bass
import concourse.mybir as mybir
from concourse.bass_utils import run_bass_kernel_spmd

F32 = mybir.dt.float32
BF16 = mybir.dt.bfloat16
I16 = mybir.dt.int16
AF = mybir.ActivationFunctionType
ALU = mybir.AluOpType
BF = ml_dtypes.bfloat16

B, S, E = 4, 2048, 2048
H, G, D = 16, 4, 128
NE = 16
HID = 8192
LN_EPS = 1e-5
RMS_EPS = 1.1920928955078125e-07
ISQ = float(1.0 / np.sqrt(128.0))

ENGS = ("tensor", "vector", "scalar", "gpsimd", "sync")
DEBUG = False
TRACE = None


class Em:

    def __init__(self, cnt):
        self.q = {e: [] for e in ENGS}
        self.cnt = cnt

    def do(self, eng, fn, inc=None):
        if TRACE is not None:
            import traceback
            tag = traceback.extract_stack(limit=3)[0].lineno
            TRACE.append(("do", eng, inc[0] if inc else None,
                          self.cnt[inc[0]] + inc[1] if inc else None, tag))
        if inc is not None:
            name, amt = inc
            self.cnt[name] += amt
            self.q[eng].append((fn, name, amt))
            return self.cnt[name]
        self.q[eng].append((fn, None, 0))
        return None

    def wait(self, eng, name, val=None):
        v = self.cnt[name] if val is None else val
        if TRACE is not None:
            import traceback
            tag = traceback.extract_stack(limit=3)[0].lineno
            TRACE.append(("wait", eng, name, v, tag))
        if v <= 0:
            return
        self.q[eng].append((("wait", name, v), None, 0))

    def barrier(self):
        for e in ENGS:
            for s in list(self.cnt):
                self.wait(e, s)

    def replay(self, block, semh):
        def mk(eng_name):
            items = list(self.q[eng_name])

            def body(EO):
                for fn, sname, amt in items:
                    if isinstance(fn, tuple) and fn[0] == "wait":
                        EO.wait_ge(semh[fn[1]], fn[2])
                    else:
                        inst = fn(EO)
                        if sname is not None:
                            inst.then_inc(semh[sname], amt)
            return body

        block.tensor(mk("tensor"))
        block.vector(mk("vector"))
        block.scalar(mk("scalar"))
        block.gpsimd(mk("gpsimd"))
        block.sync(mk("sync"))


def build_nc():
    nc = bass.Bass("TRN2", target_bir_lowering=False)

    d_xq = nc.dram_tensor("xq", [E, 1024], F32, kind="ExternalInput").ap()
    d_xlo = nc.dram_tensor("xlo", [E, 1024], F32, kind="ExternalInput").ap()
    d_xhi = nc.dram_tensor("xhi", [E, 2048], F32, kind="ExternalInput").ap()
    d_wq = nc.dram_tensor("wqt", [H, 128, 2048], F32, kind="ExternalInput").ap()
    d_wk = nc.dram_tensor("wkt", [G, 128, 2048], F32, kind="ExternalInput").ap()
    d_wv = nc.dram_tensor("wvt", [G, 128, 2048], F32, kind="ExternalInput").ap()
    d_bqt = nc.dram_tensor("bqt", [128, H], F32, kind="ExternalInput").ap()
    d_bkt = nc.dram_tensor("bkt", [128, G], F32, kind="ExternalInput").ap()
    d_bvr = nc.dram_tensor("bvr", [128, G * 128], F32, kind="ExternalInput").ap()
    d_msk = nc.dram_tensor("msk", [6, 128, 512], BF16, kind="ExternalInput").ap()
    d_gu = nc.dram_tensor("gut", [64, 128, 4096], F32, kind="ExternalInput").ap()
    d_wp = nc.dram_tensor("wpt", [32, 128, 4096], F32, kind="ExternalInput").ap()
    d_out = nc.dram_tensor("out", [E, 1024], F32, kind="ExternalOutput").ap()
    d_xsp = nc.dram_tensor("xsp", [E, 1024], F32,
                           kind="ExternalOutput" if DEBUG else "Internal").ap()
    d_whg = nc.dram_tensor("whg", [64, 128, 4096], BF16).ap()
    d_whp = nc.dram_tensor("whp", [32, 128, 4096], BF16).ap()
    if DEBUG:
        d_dq = nc.dram_tensor("dbg_q", [128, H * 1024], BF16, kind="ExternalOutput").ap()
        d_dkl = nc.dram_tensor("dbg_klo", [128, G * 1024], BF16, kind="ExternalOutput").ap()
        d_dkh = nc.dram_tensor("dbg_khi", [128, G * 2048], BF16, kind="ExternalOutput").ap()
        d_dvl = nc.dram_tensor("dbg_vlo", [128, G * 1024], BF16, kind="ExternalOutput").ap()
        d_dvh = nc.dram_tensor("dbg_vhi", [128, G * 2048], BF16, kind="ExternalOutput").ap()
        d_dxn2 = nc.dram_tensor("dbg_xn2", [128, NE * 1024], BF16, kind="ExternalOutput").ap()
        d_dxnq = nc.dram_tensor("dbg_xnq", [128, NE * 512], BF16, kind="ExternalOutput").ap()

    cnt = {s: 0 for s in ("SL", "GL", "SS", "TE", "VE", "SE")}

    import contextlib
    with contextlib.ExitStack() as ST:
        semh = {s: ST.enter_context(nc.semaphore(s)) for s in cnt}

        t_bqt = ST.enter_context(nc.sbuf_tensor("bqt_s", [128, H], F32))
        t_bkt = ST.enter_context(nc.sbuf_tensor("bkt_s", [128, G], F32))
        t_bvr = ST.enter_context(nc.sbuf_tensor("bvr_s", [128, G * 128], F32))
        t_ones = ST.enter_context(nc.sbuf_tensor("ones_s", [128, 256], BF16))
        t_msk = ST.enter_context(nc.sbuf_tensor("msk_s", [128, 6 * 512], BF16))
        t_xn2 = ST.enter_context(nc.sbuf_tensor("xn2_s", [128, NE * 1024], BF16))
        ps = ST.enter_context(nc.psum_tensor("pp", [128, 4096], F32))

        def PSB(b, w=512):
            return ps[:, b * 512: b * 512 + w]

        ones128 = t_ones[:, 0:128]
        ones64 = t_ones[:, 128:256]

        def MM(E_, out, lhsT, rhs, start, stop):
            return E_.matmul(out, lhsT, rhs, start=start, stop=stop,
                             skip_group_check=True)

        with contextlib.ExitStack() as SA:
            t_qall = SA.enter_context(nc.sbuf_tensor("qall", [128, H * 1024], BF16))
            t_klo = SA.enter_context(nc.sbuf_tensor("klo", [128, G * 1024], BF16))
            t_vlo = SA.enter_context(nc.sbuf_tensor("vlo", [128, G * 1024], BF16))
            t_khi = SA.enter_context(nc.sbuf_tensor("khi", [128, G * 2048], BF16))
            t_vhi = SA.enter_context(nc.sbuf_tensor("vhi", [128, G * 2048], BF16))
            t_pt = SA.enter_context(nc.sbuf_tensor("pt", [128, 8 * 512], BF16))
            t_xnb = SA.enter_context(nc.sbuf_tensor("xnb", [128, NE * 512], BF16))
            t_xnq = SA.enter_context(nc.sbuf_tensor("xnq", [128, NE * 512], BF16))
            t_wkv = SA.enter_context(nc.sbuf_tensor("wkv", [128, 2 * 2048], BF16))
            t_xf = SA.enter_context(nc.sbuf_tensor("xf", [128, 2 * 512], F32))
            t_st = SA.enter_context(nc.sbuf_tensor("stt", [128, 3 * 512], F32))
            t_rstd = SA.enter_context(nc.sbuf_tensor("rstd", [128, 512], BF16))
            t_xc = SA.enter_context(nc.sbuf_tensor("xc", [128, 2 * 512], F32))
            t_oc = SA.enter_context(nc.sbuf_tensor("oc", [128, 2 * 512], F32))
            t_wq2 = SA.enter_context(nc.sbuf_tensor("wq2", [128, 2 * 2048], BF16))
            t_cst = SA.enter_context(nc.sbuf_tensor("cst", [128, 4 * 512], BF16))

            em = Em(cnt)

            em.do("sync", lambda E_: E_.dma_start(out=t_bqt[:, :], in_=d_bqt), inc=("SL", 16))
            em.do("sync", lambda E_: E_.dma_start(out=t_bkt[:, :], in_=d_bkt), inc=("SL", 16))
            em.do("sync", lambda E_: E_.dma_start(out=t_bvr[:, :], in_=d_bvr), inc=("SL", 16))
            for mi in range(6):
                em.do("sync", lambda E_, mi=mi: E_.dma_start(
                    out=t_msk[:, mi * 512:(mi + 1) * 512], in_=d_msk[mi]), inc=("SL", 16))
            em.do("vector", lambda E_: E_.memset(ones128, 1.0), inc=("VE", 1))
            em.do("vector", lambda E_: E_.memset(ones64, 0.0), inc=("VE", 1))
            em.do("vector", lambda E_: E_.memset(t_ones[0:64, 128:192], 1.0), inc=("VE", 1))
            em.do("vector", lambda E_: E_.memset(t_ones[64:128, 192:256], 1.0), inc=("VE", 1))

            def MSK(i):
                return t_msk[:, i * 512:(i + 1) * 512]

            ln_prev = [0, 0]

            def ln_emit(src_fn, dst_fn):
                te_marks = {}
                for e in range(NE):
                    xf = t_xf[:, (e % 2) * 512:(e % 2) * 512 + 512]
                    em.wait("sync", "VE")
                    if e >= 2:
                        em.wait("sync", "SE")
                    em.do("sync", lambda E_, e=e, xf=xf: E_.dma_start(
                        out=xf, in_=src_fn(e)), inc=("SL", 16))
                    lv = cnt["SL"]
                    cast = t_cst[:, (e % 2) * 512:(e % 2) * 512 + 512]
                    sq = t_cst[:, 1024 + (e % 2) * 512: 1024 + (e % 2) * 512 + 512]
                    guard = te_marks[e - 2] if e >= 2 else ln_prev[e]
                    em.wait("vector", "SL", lv)
                    if guard:
                        em.wait("vector", "TE", guard)
                    em.do("vector", lambda E_, xf=xf, cast=cast: E_.tensor_copy(cast, xf),
                          inc=("VE", 1))
                    em.wait("scalar", "SL", lv)
                    if guard:
                        em.wait("scalar", "TE", guard)
                    em.do("scalar", lambda E_, xf=xf, sq=sq: E_.activation(sq, xf, AF.Square),
                          inc=("SE", 1))
                    em.wait("tensor", "VE")
                    em.wait("tensor", "SE")
                    em.do("tensor", lambda E_, e=e, cast=cast: MM(
                        E_, PSB(0), ones128, cast, e == 0, e == NE - 1))
                    te_marks[e] = em.do("tensor", lambda E_, e=e, sq=sq: MM(
                        E_, PSB(4), ones128, sq, e == 0, e == NE - 1), inc=("TE", 1))
                st_mean = t_st[:, 0:512]
                st_wk = t_st[:, 512:1024]
                em.wait("vector", "TE")
                em.do("vector", lambda E_: E_.tensor_scalar(
                    st_mean, PSB(0), 1.0 / E, None, ALU.mult), inc=("VE", 1))
                em.do("vector", lambda E_: E_.tensor_tensor(
                    st_wk, st_mean, st_mean, ALU.mult), inc=("VE", 1))
                em.do("vector", lambda E_: E_.scalar_tensor_tensor(
                    st_wk, PSB(4), 1.0 / E, st_wk, ALU.mult, ALU.subtract), inc=("VE", 1))
                em.do("vector", lambda E_: E_.tensor_scalar(
                    st_wk, st_wk, LN_EPS, None, ALU.add), inc=("VE", 1))
                em.do("vector", lambda E_: E_.reciprocal(st_wk, st_wk), inc=("VE", 1))
                em.wait("scalar", "VE")
                em.do("scalar", lambda E_: E_.activation(t_rstd[:, :], st_wk, AF.Sqrt),
                      inc=("SE", 1))
                se_rstd = cnt["SE"]
                em.wait("sync", "SE")
                for e in range(NE):
                    xf = t_xf[:, (e % 2) * 512:(e % 2) * 512 + 512]
                    em.wait("sync", "VE")
                    em.do("sync", lambda E_, e=e, xf=xf: E_.dma_start(
                        out=xf, in_=src_fn(e)), inc=("SL", 16))
                    lv = cnt["SL"]
                    tmp = t_cst[:, (e % 2) * 512:(e % 2) * 512 + 512]
                    em.wait("vector", "SL", lv)
                    if e == 0:
                        em.wait("vector", "SE", se_rstd)
                    em.do("vector", lambda E_, xf=xf, tmp=tmp: E_.tensor_tensor(
                        tmp, xf, st_mean, ALU.subtract), inc=("VE", 1))
                    em.do("vector", lambda E_, tmp=tmp, e=e: E_.tensor_tensor(
                        dst_fn(e), tmp, t_rstd[:, :], ALU.mult), inc=("VE", 1))
                ln_prev[0] = te_marks[NE - 2]
                ln_prev[1] = te_marks[NE - 1]

            qte = {}
            for hf in range(2):
                ln_emit(lambda e, hf=hf: d_xq[e * 128:(e + 1) * 128, hf * 512:hf * 512 + 512],
                        lambda e: t_xnq[:, e * 512:(e + 1) * 512])
                for h in range(H):
                    wbuf = t_wq2[:, (h % 2) * 2048:(h % 2) * 2048 + 2048]
                    key = (hf, h)
                    prev = qte.get((hf, h - 2)) if h >= 2 else (qte.get((hf - 1, H - 1)) if hf > 0 else None)
                    if prev is not None:
                        em.wait("gpsimd", "TE", prev)
                    em.do("gpsimd", lambda E_, h=h, wbuf=wbuf: E_.dma_start(
                        out=wbuf, in_=d_wq[h]), inc=("GL", 16))
                    gl = cnt["GL"]
                    em.wait("tensor", "GL", gl)
                    em.wait("tensor", "VE")
                    for e in range(NE):
                        mark = em.do("tensor", lambda E_, e=e, wbuf=wbuf: MM(
                            E_, PSB(1), wbuf[:, e * 128:(e + 1) * 128],
                            t_xnq[:, e * 512:(e + 1) * 512], e == 0, e == NE - 1),
                            inc=("TE", 1) if e == NE - 1 else None)
                    qte[key] = mark
                    em.wait("vector", "TE", mark)
                    em.do("vector", lambda E_, h=h, hf=hf: E_.tensor_scalar(
                        t_qall[:, h * 1024 + hf * 512: h * 1024 + hf * 512 + 512],
                        PSB(1), t_bqt[:, h:h + 1], None, ALU.add), inc=("VE", 1))

            def kv_emit(nblk, xsrc, kdst, vdst):
                for kb in range(nblk):
                    ln_emit(lambda e, kb=kb: xsrc[e * 128:(e + 1) * 128, kb * 512:kb * 512 + 512],
                            lambda e: t_xnb[:, e * 512:(e + 1) * 512])
                    for g in range(G):
                        wkb = t_wkv[:, 0:2048]
                        wvb = t_wkv[:, 2048:4096]
                        em.wait("gpsimd", "TE")
                        em.do("gpsimd", lambda E_, g=g, wkb=wkb: E_.dma_start(
                            out=wkb, in_=d_wk[g]), inc=("GL", 16))
                        em.do("gpsimd", lambda E_, g=g, wvb=wvb: E_.dma_start(
                            out=wvb, in_=d_wv[g]), inc=("GL", 16))
                        gl = cnt["GL"]
                        em.wait("tensor", "GL", gl)
                        em.wait("tensor", "VE")
                        for e in range(NE):
                            em.do("tensor", lambda E_, e=e, wkb=wkb: MM(
                                E_, PSB(0), wkb[:, e * 128:(e + 1) * 128],
                                t_xnb[:, e * 512:(e + 1) * 512], e == 0, e == NE - 1),
                                inc=("TE", 1) if e == NE - 1 else None)
                        for vc in range(4):
                            for e in range(NE):
                                em.do("tensor", lambda E_, e=e, vc=vc, wvb=wvb: MM(
                                    E_, ps[:, 1024 + vc * 128:1024 + vc * 128 + 128],
                                    t_xnb[:, e * 512 + vc * 128: e * 512 + vc * 128 + 128],
                                    wvb[:, e * 128:(e + 1) * 128], e == 0, e == NE - 1),
                                    inc=("TE", 1) if (e == NE - 1 and vc == 3) else None)
                        te = cnt["TE"]
                        em.wait("vector", "TE", te)
                        em.do("vector", lambda E_, g=g, kb=kb: E_.tensor_scalar(
                            kdst(g)[:, kb * 512:(kb + 1) * 512], PSB(0),
                            t_bkt[:, g:g + 1], None, ALU.add), inc=("VE", 1))
                        for vc in range(4):
                            em.do("vector", lambda E_, g=g, kb=kb, vc=vc: E_.tensor_tensor(
                                vdst(g)[:, (kb * 4 + vc) * 128:(kb * 4 + vc) * 128 + 128],
                                ps[:, 1024 + vc * 128:1024 + vc * 128 + 128],
                                t_bvr[:, g * 128:(g + 1) * 128], ALU.add), inc=("VE", 1))

            kv_emit(2, d_xlo, lambda g: t_klo[:, g * 1024:(g + 1) * 1024],
                    lambda g: t_vlo[:, g * 1024:(g + 1) * 1024])
            kv_emit(4, d_xhi, lambda g: t_khi[:, g * 2048:(g + 1) * 2048],
                    lambda g: t_vhi[:, g * 2048:(g + 1) * 2048])

            em.wait("tensor", "VE")
            for h in range(H):
                g = h // 4
                for qt in range(2):
                    nw = 8 if qt == 0 else 16
                    kt = t_klo[:, g * 1024:(g + 1) * 1024] if qt == 0 else t_khi[:, g * 2048:(g + 1) * 2048]
                    vt = t_vlo[:, g * 1024:(g + 1) * 1024] if qt == 0 else t_vhi[:, g * 2048:(g + 1) * 2048]
                    qrhs = t_qall[:, h * 1024 + qt * 512: h * 1024 + qt * 512 + 512]
                    npair = nw // 2
                    exp_se = {}
                    avte = {}
                    xcb = t_xc[:, qt * 512:qt * 512 + 512]
                    em.wait("sync", "VE")
                    em.do("sync", lambda E_, h=h, qt=qt, xcb=xcb: E_.dma_start(
                        out=xcb, in_=d_xq[h * 128:(h + 1) * 128, qt * 512:(qt + 1) * 512]),
                        inc=("SL", 16))
                    xl = cnt["SL"]
                    em.wait("tensor", "SE")
                    for i in range(npair):
                        bsel = 4 + 2 * (i % 2)
                        if i >= 2:
                            em.wait("tensor", "SE", exp_se[i - 2])
                        for j in range(2):
                            w = 2 * i + j
                            em.do("tensor", lambda E_, w=w, bsel=bsel, j=j, kt=kt, qrhs=qrhs: MM(
                                E_, PSB(bsel + j), kt[:, w * 128:(w + 1) * 128], qrhs,
                                True, True), inc=("TE", 1) if j == 1 else None)
                        sc_te = cnt["TE"]
                        em.wait("scalar", "TE", sc_te)
                        if i >= 4:
                            em.wait("scalar", "TE", avte[i - 4])
                        em.do("scalar", lambda E_, i=i, bsel=bsel: E_.activation(
                            t_pt[:, (2 * i % 8) * 512: (2 * i % 8) * 512 + 1024],
                            ps[:, bsel * 512: bsel * 512 + 1024], AF.Exp, scale=ISQ),
                            inc=("SE", 1))
                        exp_se[i] = cnt["SE"]
                        masked = []
                        for j in range(2):
                            w = 2 * i + j
                            mi = None
                            if qt == 0:
                                mi = 0 if w < 4 else 2 + (w - 4)
                            else:
                                if w < 4:
                                    mi = 1
                                elif w >= 12:
                                    mi = 2 + (w - 12)
                            if mi is not None:
                                masked.append((w, mi))
                        if masked:
                            em.wait("vector", "SE", exp_se[i])
                            for w, mi in masked:
                                em.do("vector", lambda E_, w=w, mi=mi: E_.tensor_tensor(
                                    t_pt[:, (w % 8) * 512:(w % 8) * 512 + 512],
                                    t_pt[:, (w % 8) * 512:(w % 8) * 512 + 512],
                                    MSK(mi), ALU.mult), inc=("VE", 1))
                            em.wait("tensor", "VE", cnt["VE"])
                        else:
                            em.wait("tensor", "SE", exp_se[i])
                        for j in range(2):
                            w = 2 * i + j
                            em.do("tensor", lambda E_, w=w, nw=nw: MM(
                                E_, PSB(2), ones128,
                                t_pt[:, (w % 8) * 512:(w % 8) * 512 + 512],
                                w == 0, w == nw - 1))
                        for j in range(2):
                            w = 2 * i + j
                            em.do("tensor", lambda E_, w=w, nw=nw, vt=vt: MM(
                                E_, PSB(3), vt[:, w * 128:(w + 1) * 128],
                                t_pt[:, (w % 8) * 512:(w % 8) * 512 + 512],
                                w == 0, w == nw - 1), inc=("TE", 1) if j == 1 else None)
                        avte[i] = cnt["TE"]
                    em.wait("vector", "TE", avte[npair - 1])
                    em.do("vector", lambda E_: E_.reciprocal(t_st[:, 0:512], PSB(2)),
                          inc=("VE", 1))
                    em.do("vector", lambda E_: E_.tensor_tensor(
                        t_st[:, 512:1024], PSB(3), t_st[:, 0:512], ALU.mult), inc=("VE", 1))
                    em.wait("vector", "SL", xl)
                    em.wait("vector", "SS")
                    ocb = t_oc[:, qt * 512:qt * 512 + 512]
                    em.do("vector", lambda E_, xcb=xcb, ocb=ocb: E_.tensor_tensor(
                        ocb, t_st[:, 512:1024], xcb, ALU.add), inc=("VE", 1))
                    ove = cnt["VE"]
                    em.wait("sync", "VE", ove)
                    em.do("sync", lambda E_, h=h, qt=qt, ocb=ocb: E_.dma_start(
                        out=d_xsp[h * 128:(h + 1) * 128, qt * 512:(qt + 1) * 512], in_=ocb),
                        inc=("SS", 16))

            em.wait("sync", "SS")
            em.barrier()
            if DEBUG:
                em.do("sync", lambda E_: E_.dma_start(out=d_dq, in_=t_qall[:, :]), inc=("SS", 16))
                em.do("sync", lambda E_: E_.dma_start(out=d_dkl, in_=t_klo[:, :]), inc=("SS", 16))
                em.do("sync", lambda E_: E_.dma_start(out=d_dkh, in_=t_khi[:, :]), inc=("SS", 16))
                em.do("sync", lambda E_: E_.dma_start(out=d_dvl, in_=t_vlo[:, :]), inc=("SS", 16))
                em.do("sync", lambda E_: E_.dma_start(out=d_dvh, in_=t_vhi[:, :]), inc=("SS", 16))
            for hf in range(2):
                ln_emit(lambda e, hf=hf: d_xsp[e * 128:(e + 1) * 128, hf * 512:hf * 512 + 512],
                        lambda e, hf=hf: t_xn2[:, e * 1024 + hf * 512: e * 1024 + hf * 512 + 512])
            if DEBUG:
                em.barrier()
                em.do("sync", lambda E_: E_.dma_start(out=d_dxn2, in_=t_xn2[:, :]), inc=("SS", 16))
            em.barrier()

            with nc.Block() as blk:
                em.replay(blk, semh)

        with contextlib.ExitStack() as SF:
            t_h = SF.enter_context(nc.sbuf_tensor("hsb", [128, 64 * 512], BF16))
            t_wbf = SF.enter_context(nc.sbuf_tensor("wbf", [128, 2 * 4096], BF16))
            t_wh = SF.enter_context(nc.sbuf_tensor("wh", [128, 2 * 4096], BF16))
            t_qa = SF.enter_context(nc.sbuf_tensor("qa", [128, 4096], BF16))
            t_qb = SF.enter_context(nc.sbuf_tensor("qb", [128, 4096], BF16))
            t_qc = SF.enter_context(nc.sbuf_tensor("qc", [128, 4096], BF16))
            t_qd = SF.enter_context(nc.sbuf_tensor("qd", [128, 4096], BF16))
            t_sg = SF.enter_context(nc.sbuf_tensor("sg", [128, 2 * 512], BF16))
            t_xo = SF.enter_context(nc.sbuf_tensor("xo", [128, 4 * 512], F32))
            t_fst = SF.enter_context(nc.sbuf_tensor("fst", [128, 2 * 512], F32))

            em = Em(cnt)
            em.barrier()

            def quant(src, dst, tile_n):
                ntile = 4096 // tile_n
                em.do("vector", lambda E_, src=src: E_.tensor_scalar(
                    t_qa[:, :].bitcast(I16), src.bitcast(I16), 0x7FFF, None,
                    ALU.bitwise_and), inc=("VE", 1))
                ve_abs = cnt["VE"]
                half = ntile // 2
                for rnd in range(2):
                    em.wait("tensor", "VE", ve_abs)
                    em.wait("tensor", "SE")
                    for it in range(half):
                        c0 = (rnd * half + it) * tile_n
                        em.do("tensor", lambda E_, c0=c0, it=it, tile_n=tile_n: MM(
                            E_, ps[:, it * tile_n: it * tile_n + tile_n], ones64,
                            t_qa[:, c0:c0 + tile_n], True, True),
                            inc=("TE", 1) if it == half - 1 else None)
                    te = cnt["TE"]
                    em.wait("scalar", "TE", te)
                    em.do("scalar", lambda E_, rnd=rnd: E_.mul(
                        t_qc[:, rnd * 2048:(rnd + 1) * 2048], ps[:, 0:2048], 1.0 / 128),
                        inc=("SE", 1))
                em.wait("vector", "SE")
                em.do("vector", lambda E_: E_.tensor_tensor(
                    t_qb[:, :], t_qa[:, :], t_qc[:, :], ALU.is_gt), inc=("VE", 1))
                em.do("vector", lambda E_, src=src: E_.tensor_scalar(
                    t_qd[:, :].bitcast(I16), src.bitcast(I16), -0x8000, None,
                    ALU.bitwise_and), inc=("VE", 1))
                em.do("vector", lambda E_: E_.scalar_tensor_tensor(
                    t_qa[:, :], t_qb[:, :], 2.0, t_qc[:, :], ALU.mult, ALU.mult),
                    inc=("VE", 1))
                em.wait("vector", "SS")
                em.do("vector", lambda E_, dst=dst: E_.tensor_tensor(
                    dst.bitcast(I16), t_qa[:, :].bitcast(I16), t_qd[:, :].bitcast(I16),
                    ALU.bitwise_or), inc=("VE", 1))

            for p in range(2):
                for j in range(64):
                    wb = t_wbf[:, (j % 2) * 4096:(j % 2) * 4096 + 4096]
                    wh = t_wh[:, (j % 2) * 4096:(j % 2) * 4096 + 4096]
                    if p == 0:
                        if j >= 2:
                            em.wait("gpsimd", "VE")
                        em.do("gpsimd", lambda E_, j=j, wb=wb: E_.dma_start(
                            out=wb, in_=d_gu[j]), inc=("GL", 16))
                        gl = cnt["GL"]
                        em.wait("vector", "GL", gl)
                        if j >= 2:
                            em.wait("vector", "TE")
                        quant(wb, wh, 256)
                        ve_w = cnt["VE"]
                        em.wait("sync", "VE", ve_w)
                        em.do("sync", lambda E_, j=j, wh=wh: E_.dma_start(
                            out=d_whg[j], in_=wh), inc=("SS", 16))
                        em.wait("tensor", "VE", ve_w)
                    else:
                        if j >= 2:
                            em.wait("sync", "TE")
                        em.do("sync", lambda E_, j=j, wh=wh: E_.dma_start(
                            out=wh, in_=d_whg[j]), inc=("SL", 16))
                        em.wait("tensor", "SL", cnt["SL"])
                    if j >= 1:
                        em.wait("tensor", "VE")
                        em.wait("tensor", "SE")
                    for kc in range(NE):
                        em.do("tensor", lambda E_, kc=kc, wh=wh, p=p: MM(
                            E_, PSB(4), wh[:, kc * 256: kc * 256 + 128],
                            t_xn2[:, kc * 1024 + p * 512: kc * 1024 + p * 512 + 512],
                            kc == 0, kc == NE - 1))
                        em.do("tensor", lambda E_, kc=kc, wh=wh, p=p: MM(
                            E_, PSB(5), wh[:, kc * 256 + 128: kc * 256 + 256],
                            t_xn2[:, kc * 1024 + p * 512: kc * 1024 + p * 512 + 512],
                            kc == 0, kc == NE - 1),
                            inc=("TE", 1) if kc == NE - 1 else None)
                    gu_te = cnt["TE"]
                    em.wait("scalar", "TE", gu_te)
                    em.wait("scalar", "VE")
                    em.do("scalar", lambda E_: E_.activation(
                        t_sg[:, 0:512], PSB(4), AF.Silu), inc=("SE", 1))
                    em.wait("vector", "SE")
                    em.do("vector", lambda E_, j=j: E_.tensor_tensor(
                        t_h[:, j * 512:(j + 1) * 512], PSB(5), t_sg[:, 0:512], ALU.mult),
                        inc=("VE", 1))
                    em.wait("vector", "TE")
                    em.do("vector", lambda E_, j=j: E_.tensor_tensor(
                        t_sg[:, 512:1024], t_h[:, j * 512:(j + 1) * 512],
                        t_h[:, j * 512:(j + 1) * 512], ALU.mult), inc=("VE", 1))
                    hve = cnt["VE"]
                    em.wait("tensor", "VE", hve)
                    em.do("tensor", lambda E_, j=j: MM(
                        E_, PSB(6), ones128, t_sg[:, 512:1024], j == 0, j == 63),
                        inc=("TE", 1))
                em.wait("vector", "TE")
                em.do("vector", lambda E_: E_.tensor_scalar(
                    t_fst[:, 0:512], PSB(6), 1.0 / HID, RMS_EPS, ALU.mult, ALU.add),
                    inc=("VE", 1))
                em.do("vector", lambda E_: E_.reciprocal(t_fst[:, 0:512], t_fst[:, 0:512]),
                      inc=("VE", 1))
                em.wait("scalar", "VE")
                em.do("scalar", lambda E_: E_.activation(
                    t_fst[:, 512:1024], t_fst[:, 0:512], AF.Sqrt), inc=("SE", 1))
                rsq_se = cnt["SE"]
                for dc in range(16):
                    xcb = t_xo[:, (dc % 2) * 512:(dc % 2) * 512 + 512]
                    em.wait("sync", "VE")
                    em.do("sync", lambda E_, dc=dc, p=p, xcb=xcb: E_.dma_start(
                        out=xcb, in_=d_xsp[dc * 128:(dc + 1) * 128, p * 512:(p + 1) * 512]),
                        inc=("SL", 16))
                    xl = cnt["SL"]
                    em.wait("tensor", "VE")
                    for half in range(2):
                        idx = dc * 2 + half
                        wb = t_wbf[:, (idx % 2) * 4096:(idx % 2) * 4096 + 4096]
                        wh = t_wh[:, (idx % 2) * 4096:(idx % 2) * 4096 + 4096]
                        if p == 0:
                            em.wait("gpsimd", "VE")
                            em.do("gpsimd", lambda E_, idx=idx, wb=wb: E_.dma_start(
                                out=wb, in_=d_wp[idx]), inc=("GL", 16))
                            gl = cnt["GL"]
                            em.wait("vector", "GL", gl)
                            em.wait("vector", "TE")
                            quant(wb, wh, 128)
                            ve_w = cnt["VE"]
                            em.wait("sync", "VE", ve_w)
                            em.do("sync", lambda E_, idx=idx, wh=wh: E_.dma_start(
                                out=d_whp[idx], in_=wh), inc=("SS", 16))
                            em.wait("tensor", "VE", ve_w)
                        else:
                            em.wait("sync", "TE")
                            em.do("sync", lambda E_, idx=idx, wh=wh: E_.dma_start(
                                out=wh, in_=d_whp[idx]), inc=("SL", 16))
                            em.wait("tensor", "SL", cnt["SL"])
                        for hc in range(32):
                            hcg = half * 32 + hc
                            em.do("tensor", lambda E_, hc=hc, hcg=hcg, wh=wh: MM(
                                E_, PSB(7), wh[:, hc * 128:(hc + 1) * 128],
                                t_h[:, hcg * 512:(hcg + 1) * 512], hcg == 0, hcg == 63),
                                inc=("TE", 1) if (half == 1 and hc == 31) else None)
                    pj_te = cnt["TE"]
                    em.wait("vector", "TE", pj_te)
                    em.wait("vector", "SE", rsq_se)
                    em.wait("vector", "SS")
                    em.do("vector", lambda E_: E_.tensor_tensor(
                        t_xo[:, 1024:1536], PSB(7), t_fst[:, 512:1024], ALU.mult),
                        inc=("VE", 1))
                    em.wait("vector", "SL", xl)
                    em.do("vector", lambda E_, xcb=xcb: E_.tensor_tensor(
                        t_xo[:, 1536:2048], t_xo[:, 1024:1536], xcb, ALU.add),
                        inc=("VE", 1))
                    ove = cnt["VE"]
                    em.wait("sync", "VE", ove)
                    em.do("sync", lambda E_, dc=dc, p=p: E_.dma_start(
                        out=d_out[dc * 128:(dc + 1) * 128, p * 512:(p + 1) * 512],
                        in_=t_xo[:, 1536:2048]), inc=("SS", 16))
                em.barrier()

            em.wait("sync", "SS")

            with nc.Block() as blk:
                em.replay(blk, semh)

    return nc


def stage_inputs(L0_attn_x, L0_attn_Wq, L0_attn_bq, L0_attn_Wk, L0_attn_bk,
                 L0_attn_Wv, L0_attn_bv, L0_ffn_w_gate_up, L0_ffn_w_proj):
    x = np.asarray(L0_attn_x, dtype=np.float32)
    xT = np.ascontiguousarray(x.transpose(0, 2, 1))

    wq_t = np.ascontiguousarray(
        np.asarray(L0_attn_Wq, np.float32).reshape(H, NE, 128, 128).transpose(0, 2, 1, 3)
        .reshape(H, 128, NE * 128))
    wk_t = np.ascontiguousarray(
        np.asarray(L0_attn_Wk, np.float32).reshape(G, NE, 128, 128).transpose(0, 2, 1, 3)
        .reshape(G, 128, NE * 128))
    wv_t = np.ascontiguousarray(
        np.asarray(L0_attn_Wv, np.float32).reshape(G, NE, 128, 128).transpose(0, 2, 1, 3)
        .reshape(G, 128, NE * 128))
    bqt = np.ascontiguousarray(np.asarray(L0_attn_bq, np.float32).T)
    bkt = np.ascontiguousarray(np.asarray(L0_attn_bk, np.float32).T)
    bvr = np.ascontiguousarray(
        np.broadcast_to(np.asarray(L0_attn_bv, np.float32)[:, None, :], (G, 128, 128))
        .transpose(1, 0, 2).reshape(128, G * 128))

    wgu = np.asarray(L0_ffn_w_gate_up, np.float32)
    wt3 = np.ascontiguousarray(wgu.T).reshape(NE, 128, 2 * HID)
    gate = wt3[:, :, :HID].reshape(NE, 128, 64, 128)
    up = wt3[:, :, HID:].reshape(NE, 128, 64, 128)
    stk = np.stack([gate, up], axis=3)
    gu_tiles = np.ascontiguousarray(stk.transpose(2, 1, 0, 3, 4).reshape(64, 128, 4096))

    wp = np.asarray(L0_ffn_w_proj, np.float32)
    arr = np.ascontiguousarray(wp.T).reshape(2, 32, 128, 16, 128)
    wp_tiles = np.ascontiguousarray(
        arr.transpose(3, 0, 2, 1, 4).reshape(32, 128, 4096))

    kk = np.arange(128)[:, None]
    qq = np.arange(512)[None, :]
    diag = [(kk + 128 * i <= qq).astype(BF) for i in range(4)]

    in_maps = []
    meta = []
    for core in range(8):
        b, p = core // 2, core % 2
        q0lo = 512 * p
        q0hi = 1536 - 512 * p
        own = np.concatenate([xT[b][:, q0lo:q0lo + 512], xT[b][:, q0hi:q0hi + 512]], axis=1)
        xlo = np.zeros((E, 1024), np.float32)
        lo0 = q0lo - 512
        s0, s1 = max(0, lo0), min(S, lo0 + 1024)
        xlo[:, s0 - lo0:s1 - lo0] = xT[b][:, s0:s1]
        xhi = np.zeros((E, 2048), np.float32)
        hi0 = q0hi - 1536
        s0, s1 = max(0, hi0), min(S, hi0 + 2048)
        xhi[:, s0 - hi0:s1 - hi0] = xT[b][:, s0:s1]
        prelo = np.full((128, 512), 0.0 if p == 0 else 1.0, BF)
        prehi = np.full((128, 512), 1.0 if p == 0 else 0.0, BF)
        msk = np.ascontiguousarray(np.stack([prelo, prehi] + diag))
        in_maps.append({
            "xq": np.ascontiguousarray(own),
            "xlo": xlo, "xhi": xhi,
            "wqt": wq_t, "wkt": wk_t, "wvt": wv_t,
            "bqt": bqt, "bkt": bkt, "bvr": bvr,
            "msk": msk, "gut": gu_tiles, "wpt": wp_tiles,
        })
        meta.append((b, q0lo, q0hi))
    return in_maps, meta


def kernel(**inputs):
    in_maps, meta = stage_inputs(**inputs)
    nc = build_nc()
    res = run_bass_kernel_spmd(nc, in_maps, core_ids=list(range(8)))
    out = np.empty((B, S, E), np.float32)
    for core in range(8):
        b, q0lo, q0hi = meta[core]
        o = res.results[core]["out"]
        out[b, q0lo:q0lo + 512, :] = o[:, 0:512].T
        out[b, q0hi:q0hi + 512, :] = o[:, 512:1024].T
    return out


# version 09
# speedup vs baseline: 1.9497x; 1.9497x over previous
import numpy as np
import ml_dtypes
import concourse.bass as bass
import concourse.mybir as mybir
from concourse.bass_utils import run_bass_kernel_spmd

F32 = mybir.dt.float32
BF16 = mybir.dt.bfloat16
I16 = mybir.dt.int16
AF = mybir.ActivationFunctionType
ALU = mybir.AluOpType
BF = ml_dtypes.bfloat16

B, S, E = 4, 2048, 2048
H, G, D = 16, 4, 128
NE = 16
HID = 8192
LN_EPS = 1e-5
RMS_EPS = 1.1920928955078125e-07
ISQ = float(1.0 / np.sqrt(128.0))

ENGS = ("tensor", "vector", "scalar", "gpsimd", "sync")
DEBUG = False
TRACE = None


class Em:

    def __init__(self, cnt):
        self.q = {e: [] for e in ENGS}
        self.cnt = cnt

    def do(self, eng, fn, inc=None):
        if TRACE is not None:
            import traceback
            tag = traceback.extract_stack(limit=3)[0].lineno
            TRACE.append(("do", eng, inc[0] if inc else None,
                          self.cnt[inc[0]] + inc[1] if inc else None, tag))
        if inc is not None:
            name, amt = inc
            self.cnt[name] += amt
            self.q[eng].append((fn, name, amt))
            return self.cnt[name]
        self.q[eng].append((fn, None, 0))
        return None

    def wait(self, eng, name, val=None):
        v = self.cnt[name] if val is None else val
        if TRACE is not None:
            import traceback
            tag = traceback.extract_stack(limit=3)[0].lineno
            TRACE.append(("wait", eng, name, v, tag))
        if v <= 0:
            return
        self.q[eng].append((("wait", name, v), None, 0))

    def barrier(self):
        for e in ENGS:
            for s in list(self.cnt):
                self.wait(e, s)

    def replay(self, block, semh):
        def mk(eng_name):
            items = list(self.q[eng_name])

            def body(EO):
                for fn, sname, amt in items:
                    if isinstance(fn, tuple) and fn[0] == "wait":
                        EO.wait_ge(semh[fn[1]], fn[2])
                    else:
                        inst = fn(EO)
                        if sname is not None:
                            inst.then_inc(semh[sname], amt)
            return body

        block.tensor(mk("tensor"))
        block.vector(mk("vector"))
        block.scalar(mk("scalar"))
        block.gpsimd(mk("gpsimd"))
        block.sync(mk("sync"))


def build_nc():
    nc = bass.Bass("TRN2", target_bir_lowering=False)

    d_xq = nc.dram_tensor("xq", [E, 1024], F32, kind="ExternalInput").ap()
    d_xlo = nc.dram_tensor("xlo", [E, 1024], F32, kind="ExternalInput").ap()
    d_xhi = nc.dram_tensor("xhi", [E, 2048], F32, kind="ExternalInput").ap()
    d_wq = nc.dram_tensor("wqt", [H, 128, 2048], F32, kind="ExternalInput").ap()
    d_wk = nc.dram_tensor("wkt", [G, 128, 2048], F32, kind="ExternalInput").ap()
    d_wv = nc.dram_tensor("wvt", [G, 128, 2048], F32, kind="ExternalInput").ap()
    d_bqt = nc.dram_tensor("bqt", [128, H], F32, kind="ExternalInput").ap()
    d_bkt = nc.dram_tensor("bkt", [128, G], F32, kind="ExternalInput").ap()
    d_bvr = nc.dram_tensor("bvr", [128, G * 128], F32, kind="ExternalInput").ap()
    d_msk = nc.dram_tensor("msk", [6, 128, 512], BF16, kind="ExternalInput").ap()
    d_gu = nc.dram_tensor("gut", [64, 128, 4096], F32, kind="ExternalInput").ap()
    d_wp = nc.dram_tensor("wpt", [32, 128, 4096], F32, kind="ExternalInput").ap()
    d_out = nc.dram_tensor("out", [E, 1024], F32, kind="ExternalOutput").ap()
    d_xsp = nc.dram_tensor("xsp", [E, 1024], F32,
                           kind="ExternalOutput" if DEBUG else "Internal").ap()
    d_whg = nc.dram_tensor("whg", [64, 128, 4096], BF16).ap()
    d_whp = nc.dram_tensor("whp", [32, 128, 4096], BF16).ap()
    if DEBUG:
        d_dq = nc.dram_tensor("dbg_q", [128, H * 1024], BF16, kind="ExternalOutput").ap()
        d_dkl = nc.dram_tensor("dbg_klo", [128, G * 1024], BF16, kind="ExternalOutput").ap()
        d_dkh = nc.dram_tensor("dbg_khi", [128, G * 2048], BF16, kind="ExternalOutput").ap()
        d_dvl = nc.dram_tensor("dbg_vlo", [128, G * 1024], BF16, kind="ExternalOutput").ap()
        d_dvh = nc.dram_tensor("dbg_vhi", [128, G * 2048], BF16, kind="ExternalOutput").ap()
        d_dxn2 = nc.dram_tensor("dbg_xn2", [128, NE * 1024], BF16, kind="ExternalOutput").ap()
        d_dxnq = nc.dram_tensor("dbg_xnq", [128, NE * 512], BF16, kind="ExternalOutput").ap()

    cnt = {s: 0 for s in ("SL", "GL", "SS", "TE", "VE", "SE")}

    import contextlib
    with contextlib.ExitStack() as ST:
        semh = {s: ST.enter_context(nc.semaphore(s)) for s in cnt}

        t_ones = ST.enter_context(nc.sbuf_tensor("ones_s", [128, 256], BF16))
        t_xn2 = ST.enter_context(nc.sbuf_tensor("xn2_s", [128, NE * 1024], BF16))
        ps = ST.enter_context(nc.psum_tensor("pp", [128, 4096], F32))

        def PSB(b, w=512):
            return ps[:, b * 512: b * 512 + w]

        ones128 = t_ones[:, 0:128]
        ones64 = t_ones[:, 128:256]

        def MM(E_, out, lhsT, rhs, start, stop):
            return E_.matmul(out, lhsT, rhs, start=start, stop=stop,
                             skip_group_check=True)

        with contextlib.ExitStack() as SA:
            t_bqt = SA.enter_context(nc.sbuf_tensor("bqt_s", [128, H], F32))
            t_bkt = SA.enter_context(nc.sbuf_tensor("bkt_s", [128, G], F32))
            t_bvr = SA.enter_context(nc.sbuf_tensor("bvr_s", [128, G * 128], F32))
            t_msk = SA.enter_context(nc.sbuf_tensor("msk_s", [128, 6 * 512], BF16))
            t_qall = SA.enter_context(nc.sbuf_tensor("qall", [128, H * 1024], BF16))
            t_klo = SA.enter_context(nc.sbuf_tensor("klo", [128, G * 1024], BF16))
            t_vlo = SA.enter_context(nc.sbuf_tensor("vlo", [128, G * 1024], BF16))
            t_khi = SA.enter_context(nc.sbuf_tensor("khi", [128, G * 2048], BF16))
            t_vhi = SA.enter_context(nc.sbuf_tensor("vhi", [128, G * 2048], BF16))
            t_pt = SA.enter_context(nc.sbuf_tensor("pt", [128, 8 * 512], BF16))
            t_xnb = SA.enter_context(nc.sbuf_tensor("xnb", [128, NE * 512], BF16))
            t_xnq = SA.enter_context(nc.sbuf_tensor("xnq", [128, NE * 512], BF16))
            t_wkv = SA.enter_context(nc.sbuf_tensor("wkv", [128, 2 * 4096], BF16))
            t_xf = SA.enter_context(nc.sbuf_tensor("xf", [128, 2 * 512], F32))
            t_st = SA.enter_context(nc.sbuf_tensor("stt", [128, 3 * 512], F32))
            t_rstd = SA.enter_context(nc.sbuf_tensor("rstd", [128, 512], BF16))
            t_xc = SA.enter_context(nc.sbuf_tensor("xc", [128, 2 * 512], F32))
            t_oc = SA.enter_context(nc.sbuf_tensor("oc", [128, 2 * 512], F32))
            t_wq2 = SA.enter_context(nc.sbuf_tensor("wq2", [128, 2 * 2048], BF16))
            t_cst = SA.enter_context(nc.sbuf_tensor("cst", [128, 4 * 512], BF16))

            em = Em(cnt)

            em.do("sync", lambda E_: E_.dma_start(out=t_bqt[:, :], in_=d_bqt), inc=("SL", 16))
            em.do("sync", lambda E_: E_.dma_start(out=t_bkt[:, :], in_=d_bkt), inc=("SL", 16))
            em.do("sync", lambda E_: E_.dma_start(out=t_bvr[:, :], in_=d_bvr), inc=("SL", 16))
            for mi in range(6):
                em.do("sync", lambda E_, mi=mi: E_.dma_start(
                    out=t_msk[:, mi * 512:(mi + 1) * 512], in_=d_msk[mi]), inc=("SL", 16))
            em.do("vector", lambda E_: E_.memset(ones128, 1.0), inc=("VE", 1))
            em.do("vector", lambda E_: E_.memset(ones64, 0.0), inc=("VE", 1))
            em.do("vector", lambda E_: E_.memset(t_ones[0:64, 128:192], 1.0), inc=("VE", 1))
            em.do("vector", lambda E_: E_.memset(t_ones[64:128, 192:256], 1.0), inc=("VE", 1))

            def MSK(i):
                return t_msk[:, i * 512:(i + 1) * 512]

            ln_prev = [0, 0]
            ln_prev_ve = [0, 0]

            def ln_emit(src_fn, dst_fn):
                te_marks = {}
                cast_ve = {}
                sq_se = {}
                for e in range(NE):
                    xf = t_xf[:, (e % 2) * 512:(e % 2) * 512 + 512]
                    em.wait("sync", "VE", cast_ve[e - 2] if e >= 2 else ln_prev_ve[e])
                    if e >= 2:
                        em.wait("sync", "SE", sq_se[e - 2])
                    em.do("sync", lambda E_, e=e, xf=xf: E_.dma_start(
                        out=xf, in_=src_fn(e)), inc=("SL", 16))
                    lv = cnt["SL"]
                    cast = t_cst[:, (e % 2) * 512:(e % 2) * 512 + 512]
                    sq = t_cst[:, 1024 + (e % 2) * 512: 1024 + (e % 2) * 512 + 512]
                    guard = te_marks[e - 2] if e >= 2 else ln_prev[e]
                    em.wait("vector", "SL", lv)
                    if guard:
                        em.wait("vector", "TE", guard)
                    em.do("vector", lambda E_, xf=xf, cast=cast: E_.tensor_copy(cast, xf),
                          inc=("VE", 1))
                    cast_ve[e] = cnt["VE"]
                    em.wait("scalar", "SL", lv)
                    if guard:
                        em.wait("scalar", "TE", guard)
                    em.do("scalar", lambda E_, xf=xf, sq=sq: E_.activation(sq, xf, AF.Square),
                          inc=("SE", 1))
                    sq_se[e] = cnt["SE"]
                    em.wait("tensor", "VE", cast_ve[e])
                    em.wait("tensor", "SE", sq_se[e])
                    em.do("tensor", lambda E_, e=e, cast=cast: MM(
                        E_, PSB(0), ones128, cast, e == 0, e == NE - 1))
                    te_marks[e] = em.do("tensor", lambda E_, e=e, sq=sq: MM(
                        E_, PSB(4), ones128, sq, e == 0, e == NE - 1), inc=("TE", 1))
                st_mean = t_st[:, 0:512]
                st_wk = t_st[:, 512:1024]
                em.wait("vector", "TE")
                em.do("vector", lambda E_: E_.tensor_scalar(
                    st_mean, PSB(0), 1.0 / E, None, ALU.mult), inc=("VE", 1))
                em.do("vector", lambda E_: E_.tensor_tensor(
                    st_wk, st_mean, st_mean, ALU.mult), inc=("VE", 1))
                em.do("vector", lambda E_: E_.scalar_tensor_tensor(
                    st_wk, PSB(4), 1.0 / E, st_wk, ALU.mult, ALU.subtract), inc=("VE", 1))
                em.do("vector", lambda E_: E_.tensor_scalar(
                    st_wk, st_wk, LN_EPS, None, ALU.add), inc=("VE", 1))
                em.do("vector", lambda E_: E_.reciprocal(st_wk, st_wk), inc=("VE", 1))
                em.wait("scalar", "VE")
                em.do("scalar", lambda E_: E_.activation(t_rstd[:, :], st_wk, AF.Sqrt),
                      inc=("SE", 1))
                se_rstd = cnt["SE"]
                em.wait("sync", "SE", sq_se[NE - 1])
                sub_ve = {}
                for e in range(NE):
                    xf = t_xf[:, (e % 2) * 512:(e % 2) * 512 + 512]
                    em.wait("sync", "VE",
                            sub_ve[e - 2] if e >= 2 else cast_ve[NE - 2 + e])
                    em.do("sync", lambda E_, e=e, xf=xf: E_.dma_start(
                        out=xf, in_=src_fn(e)), inc=("SL", 16))
                    lv = cnt["SL"]
                    tmp = t_cst[:, (e % 2) * 512:(e % 2) * 512 + 512]
                    em.wait("vector", "SL", lv)
                    if e == 0:
                        em.wait("vector", "SE", se_rstd)
                    em.do("vector", lambda E_, xf=xf, tmp=tmp: E_.tensor_tensor(
                        tmp, xf, st_mean, ALU.subtract), inc=("VE", 1))
                    em.do("vector", lambda E_, tmp=tmp, e=e: E_.tensor_tensor(
                        dst_fn(e), tmp, t_rstd[:, :], ALU.mult), inc=("VE", 1))
                    sub_ve[e] = cnt["VE"]
                ln_prev[0] = te_marks[NE - 2]
                ln_prev[1] = te_marks[NE - 1]
                ln_prev_ve[0] = sub_ve[NE - 2]
                ln_prev_ve[1] = sub_ve[NE - 1]
                return cnt["VE"]

            qte = {}
            for hf in range(2):
                ln_emit(lambda e, hf=hf: d_xq[e * 128:(e + 1) * 128, hf * 512:hf * 512 + 512],
                        lambda e: t_xnq[:, e * 512:(e + 1) * 512])
                for h in range(H):
                    wbuf = t_wq2[:, (h % 2) * 2048:(h % 2) * 2048 + 2048]
                    key = (hf, h)
                    prev = qte.get((hf, h - 2)) if h >= 2 else (qte.get((hf - 1, H - 1)) if hf > 0 else None)
                    if prev is not None:
                        em.wait("gpsimd", "TE", prev)
                    em.do("gpsimd", lambda E_, h=h, wbuf=wbuf: E_.dma_start(
                        out=wbuf, in_=d_wq[h]), inc=("GL", 16))
                    gl = cnt["GL"]
                    em.wait("tensor", "GL", gl)
                    em.wait("tensor", "VE")
                    for e in range(NE):
                        mark = em.do("tensor", lambda E_, e=e, wbuf=wbuf: MM(
                            E_, PSB(1), wbuf[:, e * 128:(e + 1) * 128],
                            t_xnq[:, e * 512:(e + 1) * 512], e == 0, e == NE - 1),
                            inc=("TE", 1) if e == NE - 1 else None)
                    qte[key] = mark
                    em.wait("vector", "TE", mark)
                    em.do("vector", lambda E_, h=h, hf=hf: E_.tensor_scalar(
                        t_qall[:, h * 1024 + hf * 512: h * 1024 + hf * 512 + 512],
                        PSB(1), t_bqt[:, h:h + 1], None, ALU.add), inc=("VE", 1))

            kv_state = {"te": [0, 0], "ev": 0}

            def kv_emit(nblk, xsrc, kdst, vdst, seq0):
                for kb in range(nblk):
                    ln_ve = ln_emit(
                        lambda e, kb=kb: xsrc[e * 128:(e + 1) * 128, kb * 512:kb * 512 + 512],
                        lambda e: t_xnb[:, e * 512:(e + 1) * 512])
                    for g in range(G):
                        seq = seq0 + kb * G + g
                        par = seq % 2
                        wkb = t_wkv[:, par * 4096: par * 4096 + 2048]
                        wvb = t_wkv[:, par * 4096 + 2048: par * 4096 + 4096]
                        em.wait("gpsimd", "TE", kv_state["te"][par])
                        em.do("gpsimd", lambda E_, g=g, wkb=wkb: E_.dma_start(
                            out=wkb, in_=d_wk[g]), inc=("GL", 16))
                        em.do("gpsimd", lambda E_, g=g, wvb=wvb: E_.dma_start(
                            out=wvb, in_=d_wv[g]), inc=("GL", 16))
                        gl = cnt["GL"]
                        em.wait("tensor", "GL", gl)
                        em.wait("tensor", "VE", max(ln_ve, kv_state["ev"]))
                        for e in range(NE):
                            em.do("tensor", lambda E_, e=e, wkb=wkb: MM(
                                E_, PSB(0), wkb[:, e * 128:(e + 1) * 128],
                                t_xnb[:, e * 512:(e + 1) * 512], e == 0, e == NE - 1),
                                inc=("TE", 1) if e == NE - 1 else None)
                        for vc in range(4):
                            for e in range(NE):
                                em.do("tensor", lambda E_, e=e, vc=vc, wvb=wvb: MM(
                                    E_, ps[:, 1024 + vc * 128:1024 + vc * 128 + 128],
                                    t_xnb[:, e * 512 + vc * 128: e * 512 + vc * 128 + 128],
                                    wvb[:, e * 128:(e + 1) * 128], e == 0, e == NE - 1),
                                    inc=("TE", 1) if (e == NE - 1 and vc == 3) else None)
                        te = cnt["TE"]
                        kv_state["te"][par] = te
                        em.wait("vector", "TE", te)
                        em.do("vector", lambda E_, g=g, kb=kb: E_.tensor_scalar(
                            kdst(g)[:, kb * 512:(kb + 1) * 512], PSB(0),
                            t_bkt[:, g:g + 1], None, ALU.add), inc=("VE", 1))
                        for vc in range(4):
                            em.do("vector", lambda E_, g=g, kb=kb, vc=vc: E_.tensor_tensor(
                                vdst(g)[:, (kb * 4 + vc) * 128:(kb * 4 + vc) * 128 + 128],
                                ps[:, 1024 + vc * 128:1024 + vc * 128 + 128],
                                t_bvr[:, g * 128:(g + 1) * 128], ALU.add), inc=("VE", 1))
                        kv_state["ev"] = cnt["VE"]

            kv_emit(2, d_xlo, lambda g: t_klo[:, g * 1024:(g + 1) * 1024],
                    lambda g: t_vlo[:, g * 1024:(g + 1) * 1024], 0)
            kv_emit(4, d_xhi, lambda g: t_khi[:, g * 2048:(g + 1) * 2048],
                    lambda g: t_vhi[:, g * 2048:(g + 1) * 2048], 8)

            em.wait("tensor", "VE")
            ove_m = {0: 0, 1: 0}
            ss_m = {0: 0, 1: 0}
            for h in range(H):
                g = h // 4
                for qt in range(2):
                    nw = 8 if qt == 0 else 16
                    kt = t_klo[:, g * 1024:(g + 1) * 1024] if qt == 0 else t_khi[:, g * 2048:(g + 1) * 2048]
                    vt = t_vlo[:, g * 1024:(g + 1) * 1024] if qt == 0 else t_vhi[:, g * 2048:(g + 1) * 2048]
                    qrhs = t_qall[:, h * 1024 + qt * 512: h * 1024 + qt * 512 + 512]
                    npair = nw // 2
                    exp_se = {}
                    avte = {}
                    xcb = t_xc[:, qt * 512:qt * 512 + 512]
                    em.wait("sync", "VE", ove_m[qt])
                    em.do("sync", lambda E_, h=h, qt=qt, xcb=xcb: E_.dma_start(
                        out=xcb, in_=d_xq[h * 128:(h + 1) * 128, qt * 512:(qt + 1) * 512]),
                        inc=("SL", 16))
                    xl = cnt["SL"]
                    em.wait("tensor", "SE")
                    pw = {}

                    def emit_sumav(k):
                        em.wait("tensor", pw[k][0], pw[k][1])
                        for j in range(2):
                            w = 2 * k + j
                            em.do("tensor", lambda E_, w=w, nw=nw: MM(
                                E_, PSB(2), ones128,
                                t_pt[:, (w % 8) * 512:(w % 8) * 512 + 512],
                                w == 0, w == nw - 1))
                        for j in range(2):
                            w = 2 * k + j
                            em.do("tensor", lambda E_, w=w, nw=nw, vt=vt: MM(
                                E_, PSB(3), vt[:, w * 128:(w + 1) * 128],
                                t_pt[:, (w % 8) * 512:(w % 8) * 512 + 512],
                                w == 0, w == nw - 1), inc=("TE", 1) if j == 1 else None)
                        avte[k] = cnt["TE"]

                    for i in range(npair):
                        bsel = 4 + 2 * (i % 2)
                        if i >= 2:
                            em.wait("tensor", "SE", exp_se[i - 2])
                        for j in range(2):
                            w = 2 * i + j
                            em.do("tensor", lambda E_, w=w, bsel=bsel, j=j, kt=kt, qrhs=qrhs: MM(
                                E_, PSB(bsel + j), kt[:, w * 128:(w + 1) * 128], qrhs,
                                True, True), inc=("TE", 1) if j == 1 else None)
                        sc_te = cnt["TE"]
                        em.wait("scalar", "TE", sc_te)
                        if i >= 4:
                            em.wait("scalar", "TE", avte[i - 4])
                        em.do("scalar", lambda E_, i=i, bsel=bsel: E_.activation(
                            t_pt[:, (2 * i % 8) * 512: (2 * i % 8) * 512 + 1024],
                            ps[:, bsel * 512: bsel * 512 + 1024], AF.Exp, scale=ISQ),
                            inc=("SE", 1))
                        exp_se[i] = cnt["SE"]
                        masked = []
                        for j in range(2):
                            w = 2 * i + j
                            mi = None
                            if qt == 0:
                                mi = 0 if w < 4 else 2 + (w - 4)
                            else:
                                if w < 4:
                                    mi = 1
                                elif w >= 12:
                                    mi = 2 + (w - 12)
                            if mi is not None:
                                masked.append((w, mi))
                        if masked:
                            em.wait("vector", "SE", exp_se[i])
                            for w, mi in masked:
                                em.do("vector", lambda E_, w=w, mi=mi: E_.tensor_tensor(
                                    t_pt[:, (w % 8) * 512:(w % 8) * 512 + 512],
                                    t_pt[:, (w % 8) * 512:(w % 8) * 512 + 512],
                                    MSK(mi), ALU.mult), inc=("VE", 1))
                            pw[i] = ("VE", cnt["VE"])
                        else:
                            pw[i] = ("SE", exp_se[i])
                        if i >= 1:
                            emit_sumav(i - 1)
                    emit_sumav(npair - 1)
                    em.wait("vector", "TE", avte[npair - 1])
                    em.do("vector", lambda E_: E_.reciprocal(t_st[:, 0:512], PSB(2)),
                          inc=("VE", 1))
                    em.do("vector", lambda E_: E_.tensor_tensor(
                        t_st[:, 512:1024], PSB(3), t_st[:, 0:512], ALU.mult), inc=("VE", 1))
                    em.wait("vector", "SL", xl)
                    if ss_m[qt]:
                        em.wait("vector", "SS", ss_m[qt])
                    ocb = t_oc[:, qt * 512:qt * 512 + 512]
                    em.do("vector", lambda E_, xcb=xcb, ocb=ocb: E_.tensor_tensor(
                        ocb, t_st[:, 512:1024], xcb, ALU.add), inc=("VE", 1))
                    ove = cnt["VE"]
                    ove_m[qt] = ove
                    em.wait("sync", "VE", ove)
                    em.do("sync", lambda E_, h=h, qt=qt, ocb=ocb: E_.dma_start(
                        out=d_xsp[h * 128:(h + 1) * 128, qt * 512:(qt + 1) * 512], in_=ocb),
                        inc=("SS", 16))
                    ss_m[qt] = cnt["SS"]

            em.wait("sync", "SS")
            em.barrier()
            if DEBUG:
                em.do("sync", lambda E_: E_.dma_start(out=d_dq, in_=t_qall[:, :]), inc=("SS", 16))
                em.do("sync", lambda E_: E_.dma_start(out=d_dkl, in_=t_klo[:, :]), inc=("SS", 16))
                em.do("sync", lambda E_: E_.dma_start(out=d_dkh, in_=t_khi[:, :]), inc=("SS", 16))
                em.do("sync", lambda E_: E_.dma_start(out=d_dvl, in_=t_vlo[:, :]), inc=("SS", 16))
                em.do("sync", lambda E_: E_.dma_start(out=d_dvh, in_=t_vhi[:, :]), inc=("SS", 16))
            for hf in range(2):
                ln_emit(lambda e, hf=hf: d_xsp[e * 128:(e + 1) * 128, hf * 512:hf * 512 + 512],
                        lambda e, hf=hf: t_xn2[:, e * 1024 + hf * 512: e * 1024 + hf * 512 + 512])
            if DEBUG:
                em.barrier()
                em.do("sync", lambda E_: E_.dma_start(out=d_dxn2, in_=t_xn2[:, :]), inc=("SS", 16))
            em.barrier()

            with nc.Block() as blk:
                em.replay(blk, semh)

        with contextlib.ExitStack() as SF:
            t_h = SF.enter_context(nc.sbuf_tensor("hsb", [128, 64 * 512], BF16))
            t_wbf = SF.enter_context(nc.sbuf_tensor("wbf", [128, 3 * 4096], BF16))
            t_wh = SF.enter_context(nc.sbuf_tensor("wh", [128, 3 * 4096], BF16))
            t_qa = SF.enter_context(nc.sbuf_tensor("qa", [128, 2 * 4096], BF16))
            t_qb = SF.enter_context(nc.sbuf_tensor("qb", [128, 4096], BF16))
            t_qc = SF.enter_context(nc.sbuf_tensor("qc", [128, 2 * 4096], BF16))
            t_sg = SF.enter_context(nc.sbuf_tensor("sg", [128, 4 * 512], BF16))
            t_xo = SF.enter_context(nc.sbuf_tensor("xo", [128, 5 * 512], F32))
            t_fst = SF.enter_context(nc.sbuf_tensor("fst", [128, 2 * 512], F32))

            em = Em(cnt)
            em.barrier()

            tau_se = [0, 0]
            tau_rd = [0, 0]

            def QA(i):
                return t_qa[:, (i % 2) * 4096:(i % 2) * 4096 + 4096]

            def QC(i):
                return t_qc[:, (i % 2) * 4096:(i % 2) * 4096 + 4096]

            def quant_a(i, src_, gl_mark, tile_n):
                em.wait("vector", "GL", gl_mark)
                em.wait("vector", "SE", tau_rd_se_guard(i))
                em.do("vector", lambda E_, src_=src_, i=i: E_.tensor_scalar(
                    QA(i).bitcast(I16), src_.bitcast(I16), 0x7FFF, None,
                    ALU.bitwise_and), inc=("VE", 1))
                ve_abs = cnt["VE"]
                em.wait("scalar", "VE", tau_rd[i % 2])
                per = (4096 // tile_n) // 8
                for rnd in range(8):
                    bank = rnd % 2
                    em.wait("tensor", "VE", ve_abs)
                    em.wait("tensor", "SE", tau_se[bank])
                    for it in range(per):
                        c0 = (rnd * per + it) * tile_n
                        em.do("tensor", lambda E_, c0=c0, it=it, tile_n=tile_n, bank=bank, i=i: MM(
                            E_, ps[:, bank * 512 + it * tile_n: bank * 512 + it * tile_n + tile_n],
                            ones64, QA(i)[:, c0:c0 + tile_n], True, True),
                            inc=("TE", 1) if it == per - 1 else None)
                    te = cnt["TE"]
                    em.wait("scalar", "TE", te)
                    em.do("scalar", lambda E_, rnd=rnd, bank=bank, i=i: E_.mul(
                        QC(i)[:, rnd * 512:(rnd + 1) * 512],
                        ps[:, bank * 512: bank * 512 + 512], 1.0 / 128),
                        inc=("SE", 1))
                    tau_se[bank] = cnt["SE"]
                return cnt["SE"]

            def tau_rd_se_guard(i):
                return 0

            def quant_b(i, src_, dst, tauev, te_guard, ss_guard):
                em.wait("vector", "SE", tauev)
                em.do("vector", lambda E_, i=i: E_.tensor_tensor(
                    t_qb[:, :], QA(i), QC(i), ALU.is_gt), inc=("VE", 1))
                em.do("vector", lambda E_, i=i: E_.tensor_tensor(
                    t_qb[:, :], t_qb[:, :], QC(i), ALU.mult), inc=("VE", 1))
                em.do("vector", lambda E_: E_.tensor_scalar(
                    t_qb[:, :], t_qb[:, :], 2.0, None, ALU.mult), inc=("VE", 1))
                em.do("vector", lambda E_, src_=src_, i=i: E_.tensor_scalar(
                    QA(i).bitcast(I16), src_.bitcast(I16), -0x8000, None,
                    ALU.bitwise_and), inc=("VE", 1))
                if te_guard:
                    em.wait("vector", "TE", te_guard)
                if ss_guard:
                    em.wait("vector", "SS", ss_guard)
                em.do("vector", lambda E_, dst=dst, i=i: E_.tensor_tensor(
                    dst.bitcast(I16), t_qb[:, :].bitcast(I16), QA(i).bitcast(I16),
                    ALU.bitwise_or), inc=("VE", 1))
                tau_rd[i % 2] = cnt["VE"]
                return cnt["VE"]

            for p in range(2):
                gate_te = {}
                gu_te = {}
                q_ve = {}
                tauev = {}
                st_ss = {}
                h_ve = {}
                silu_se = {}
                sumsq_te = {}
                wh_sl = {}

                def WB(i):
                    return t_wbf[:, (i % 3) * 4096:(i % 3) * 4096 + 4096]

                def WH(i):
                    return t_wh[:, (i % 3) * 4096:(i % 3) * 4096 + 4096]

                for i in range(67):
                    if i < 64:
                        if p == 0:
                            if i >= 3:
                                em.wait("gpsimd", "VE", q_ve[i - 3])
                            em.do("gpsimd", lambda E_, i=i: E_.dma_start(
                                out=WB(i), in_=d_gu[i]), inc=("GL", 16))
                            tauev[i] = quant_a(i, WB(i), cnt["GL"], 256)
                        else:
                            if i >= 3:
                                em.wait("sync", "TE", gu_te[i - 3])
                            em.do("sync", lambda E_, i=i: E_.dma_start(
                                out=WH(i), in_=d_whg[i]), inc=("SL", 16))
                            wh_sl[i] = cnt["SL"]
                    if p == 0 and 1 <= i <= 64:
                        j = i - 1
                        q_ve[j] = quant_b(j, WB(j), WH(j), tauev[j],
                                          gu_te.get(j - 3, 0), st_ss.get(j - 3, 0))
                        em.wait("sync", "VE", q_ve[j])
                        em.do("sync", lambda E_, j=j: E_.dma_start(
                            out=d_whg[j], in_=WH(j)), inc=("SS", 16))
                        st_ss[j] = cnt["SS"]
                    if 2 <= i <= 65:
                        j = i - 2
                        wh = WH(j)
                        gbank = 4 + 3 * (j % 2)
                        ubank = 5 - 3 * (j % 2)
                        if p == 0:
                            em.wait("tensor", "VE", q_ve[j])
                        else:
                            em.wait("tensor", "SL", wh_sl[j])
                        if j >= 2:
                            em.wait("tensor", "SE", silu_se[j - 2])
                        for kc in range(NE):
                            em.do("tensor", lambda E_, kc=kc, wh=wh, p=p, gbank=gbank: MM(
                                E_, PSB(gbank), wh[:, kc * 256: kc * 256 + 128],
                                t_xn2[:, kc * 1024 + p * 512: kc * 1024 + p * 512 + 512],
                                kc == 0, kc == NE - 1),
                                inc=("TE", 1) if kc == NE - 1 else None)
                        gate_te[j] = cnt["TE"]
                        if j >= 2:
                            em.wait("tensor", "VE", h_ve[j - 2])
                        for kc in range(NE):
                            em.do("tensor", lambda E_, kc=kc, wh=wh, p=p, ubank=ubank: MM(
                                E_, PSB(ubank), wh[:, kc * 256 + 128: kc * 256 + 256],
                                t_xn2[:, kc * 1024 + p * 512: kc * 1024 + p * 512 + 512],
                                kc == 0, kc == NE - 1),
                                inc=("TE", 1) if kc == NE - 1 else None)
                        gu_te[j] = cnt["TE"]
                        sg = t_sg[:, (j % 2) * 512:(j % 2) * 512 + 512]
                        h2 = t_sg[:, 1024 + (j % 2) * 512: 1024 + (j % 2) * 512 + 512]
                        em.wait("scalar", "TE", gate_te[j])
                        if j >= 2:
                            em.wait("scalar", "VE", h_ve[j - 2])
                        em.do("scalar", lambda E_, gbank=gbank, sg=sg: E_.activation(
                            sg, PSB(gbank), AF.Silu), inc=("SE", 1))
                        silu_se[j] = cnt["SE"]
                        em.wait("vector", "SE", silu_se[j])
                        em.wait("vector", "TE", gu_te[j])
                        em.do("vector", lambda E_, j=j, sg=sg, ubank=ubank: E_.tensor_tensor(
                            t_h[:, j * 512:(j + 1) * 512], PSB(ubank), sg, ALU.mult),
                            inc=("VE", 1))
                        if j >= 2:
                            em.wait("vector", "TE", sumsq_te[j - 2])
                        em.do("vector", lambda E_, j=j, h2=h2: E_.tensor_tensor(
                            h2, t_h[:, j * 512:(j + 1) * 512],
                            t_h[:, j * 512:(j + 1) * 512], ALU.mult), inc=("VE", 1))
                        h_ve[j] = cnt["VE"]
                    if 3 <= i <= 66:
                        j = i - 3
                        h2 = t_sg[:, 1024 + (j % 2) * 512: 1024 + (j % 2) * 512 + 512]
                        em.wait("tensor", "VE", h_ve[j])
                        em.do("tensor", lambda E_, j=j, h2=h2: MM(
                            E_, PSB(6), ones128, h2, j == 0, j == 63), inc=("TE", 1))
                        sumsq_te[j] = cnt["TE"]
                em.wait("vector", "TE", sumsq_te[63])
                em.do("vector", lambda E_: E_.tensor_scalar(
                    t_fst[:, 0:512], PSB(6), 1.0 / HID, RMS_EPS, ALU.mult, ALU.add),
                    inc=("VE", 1))
                em.do("vector", lambda E_: E_.reciprocal(t_fst[:, 0:512], t_fst[:, 0:512]),
                      inc=("VE", 1))
                em.wait("scalar", "VE")
                em.do("scalar", lambda E_: E_.activation(
                    t_fst[:, 512:1024], t_fst[:, 0:512], AF.Sqrt), inc=("SE", 1))
                rsq_se = cnt["SE"]
                pj_te = {-3: gu_te[63], -2: gu_te[61], -1: gu_te[62]}
                pj_ve = {-3: q_ve.get(63, 0), -2: q_ve.get(61, 0), -1: q_ve.get(62, 0)}
                pj_ss = {-3: st_ss.get(63, 0), -2: st_ss.get(61, 0), -1: st_ss.get(62, 0)}
                pj_tauev = {}
                pj_sl = {}
                ev_ve = {}
                out_ss = {}
                hve_all = h_ve[63]
                for ii in range(34):
                    if ii < 32:
                        idx = ii
                        if (ii % 2) == 0:
                            dc = ii // 2
                            xcb = t_xo[:, (dc % 2) * 512:(dc % 2) * 512 + 512]
                            if dc >= 2:
                                em.wait("sync", "VE", ev_ve[dc - 2])
                            em.do("sync", lambda E_, dc=dc, p=p, xcb=xcb: E_.dma_start(
                                out=xcb,
                                in_=d_xsp[dc * 128:(dc + 1) * 128, p * 512:(p + 1) * 512]),
                                inc=("SL", 16))
                            pj_sl[("x", dc)] = cnt["SL"]
                        if p == 0:
                            em.wait("gpsimd", "VE", pj_ve[idx - 3])
                            em.do("gpsimd", lambda E_, idx=idx: E_.dma_start(
                                out=WB(idx), in_=d_wp[idx]), inc=("GL", 16))
                            pj_tauev[idx] = quant_a(idx, WB(idx), cnt["GL"], 128)
                        else:
                            em.wait("sync", "TE", pj_te[idx - 3])
                            em.do("sync", lambda E_, idx=idx: E_.dma_start(
                                out=WH(idx), in_=d_whp[idx]), inc=("SL", 16))
                            pj_sl[idx] = cnt["SL"]
                    if p == 0 and 1 <= ii <= 32:
                        idx = ii - 1
                        pj_ve[idx] = quant_b(idx, WB(idx), WH(idx), pj_tauev[idx],
                                             pj_te[idx - 3], pj_ss[idx - 3])
                        em.wait("sync", "VE", pj_ve[idx])
                        em.do("sync", lambda E_, idx=idx: E_.dma_start(
                            out=d_whp[idx], in_=WH(idx)), inc=("SS", 16))
                        pj_ss[idx] = cnt["SS"]
                    if 2 <= ii <= 33:
                        idx = ii - 2
                        dc, half = idx // 2, idx % 2
                        pbank = 7 - 4 * (dc % 2)
                        wh = WH(idx)
                        if p == 0:
                            em.wait("tensor", "VE", pj_ve[idx])
                        else:
                            em.wait("tensor", "SL", pj_sl[idx])
                        if idx == 0:
                            em.wait("tensor", "VE", hve_all)
                        if half == 0 and dc >= 2:
                            em.wait("tensor", "VE", ev_ve[dc - 2])
                        for hc in range(32):
                            hcg = half * 32 + hc
                            em.do("tensor", lambda E_, hc=hc, hcg=hcg, wh=wh, pbank=pbank: MM(
                                E_, PSB(pbank), wh[:, hc * 128:(hc + 1) * 128],
                                t_h[:, hcg * 512:(hcg + 1) * 512], hcg == 0, hcg == 63),
                                inc=("TE", 1) if hc == 31 else None)
                        pj_te[idx] = cnt["TE"]
                        if half == 1:
                            em.wait("vector", "TE", pj_te[idx])
                            em.wait("vector", "SE", rsq_se)
                            ob = t_xo[:, (3 + dc % 2) * 512:(3 + dc % 2) * 512 + 512]
                            em.do("vector", lambda E_, pbank=pbank: E_.tensor_tensor(
                                t_xo[:, 1024:1536], PSB(pbank), t_fst[:, 512:1024],
                                ALU.mult), inc=("VE", 1))
                            em.wait("vector", "SL", pj_sl[("x", dc)])
                            if dc >= 2:
                                em.wait("vector", "SS", out_ss[dc - 2])
                            xcb = t_xo[:, (dc % 2) * 512:(dc % 2) * 512 + 512]
                            em.do("vector", lambda E_, xcb=xcb, ob=ob: E_.tensor_tensor(
                                ob, t_xo[:, 1024:1536], xcb, ALU.add), inc=("VE", 1))
                            ev_ve[dc] = cnt["VE"]
                            em.wait("sync", "VE", ev_ve[dc])
                            em.do("sync", lambda E_, dc=dc, p=p, ob=ob: E_.dma_start(
                                out=d_out[dc * 128:(dc + 1) * 128, p * 512:(p + 1) * 512],
                                in_=ob), inc=("SS", 16))
                            out_ss[dc] = cnt["SS"]
                em.barrier()

            em.wait("sync", "SS")

            with nc.Block() as blk:
                em.replay(blk, semh)

    return nc


def stage_inputs(L0_attn_x, L0_attn_Wq, L0_attn_bq, L0_attn_Wk, L0_attn_bk,
                 L0_attn_Wv, L0_attn_bv, L0_ffn_w_gate_up, L0_ffn_w_proj):
    x = np.asarray(L0_attn_x, dtype=np.float32)
    xT = np.ascontiguousarray(x.transpose(0, 2, 1))

    wq_t = np.ascontiguousarray(
        np.asarray(L0_attn_Wq, np.float32).reshape(H, NE, 128, 128).transpose(0, 2, 1, 3)
        .reshape(H, 128, NE * 128))
    wk_t = np.ascontiguousarray(
        np.asarray(L0_attn_Wk, np.float32).reshape(G, NE, 128, 128).transpose(0, 2, 1, 3)
        .reshape(G, 128, NE * 128))
    wv_t = np.ascontiguousarray(
        np.asarray(L0_attn_Wv, np.float32).reshape(G, NE, 128, 128).transpose(0, 2, 1, 3)
        .reshape(G, 128, NE * 128))
    bqt = np.ascontiguousarray(np.asarray(L0_attn_bq, np.float32).T)
    bkt = np.ascontiguousarray(np.asarray(L0_attn_bk, np.float32).T)
    bvr = np.ascontiguousarray(
        np.broadcast_to(np.asarray(L0_attn_bv, np.float32)[:, None, :], (G, 128, 128))
        .transpose(1, 0, 2).reshape(128, G * 128))

    wgu = np.asarray(L0_ffn_w_gate_up, np.float32)
    wt3 = np.ascontiguousarray(wgu.T).reshape(NE, 128, 2 * HID)
    gate = wt3[:, :, :HID].reshape(NE, 128, 64, 128)
    up = wt3[:, :, HID:].reshape(NE, 128, 64, 128)
    stk = np.stack([gate, up], axis=3)
    gu_tiles = np.ascontiguousarray(stk.transpose(2, 1, 0, 3, 4).reshape(64, 128, 4096))

    wp = np.asarray(L0_ffn_w_proj, np.float32)
    arr = np.ascontiguousarray(wp.T).reshape(2, 32, 128, 16, 128)
    wp_tiles = np.ascontiguousarray(
        arr.transpose(3, 0, 2, 1, 4).reshape(32, 128, 4096))

    kk = np.arange(128)[:, None]
    qq = np.arange(512)[None, :]
    diag = [(kk + 128 * i <= qq).astype(BF) for i in range(4)]

    in_maps = []
    meta = []
    for core in range(8):
        b, p = core // 2, core % 2
        q0lo = 512 * p
        q0hi = 1536 - 512 * p
        own = np.concatenate([xT[b][:, q0lo:q0lo + 512], xT[b][:, q0hi:q0hi + 512]], axis=1)
        xlo = np.zeros((E, 1024), np.float32)
        lo0 = q0lo - 512
        s0, s1 = max(0, lo0), min(S, lo0 + 1024)
        xlo[:, s0 - lo0:s1 - lo0] = xT[b][:, s0:s1]
        xhi = np.zeros((E, 2048), np.float32)
        hi0 = q0hi - 1536
        s0, s1 = max(0, hi0), min(S, hi0 + 2048)
        xhi[:, s0 - hi0:s1 - hi0] = xT[b][:, s0:s1]
        prelo = np.full((128, 512), 0.0 if p == 0 else 1.0, BF)
        prehi = np.full((128, 512), 1.0 if p == 0 else 0.0, BF)
        msk = np.ascontiguousarray(np.stack([prelo, prehi] + diag))
        in_maps.append({
            "xq": np.ascontiguousarray(own),
            "xlo": xlo, "xhi": xhi,
            "wqt": wq_t, "wkt": wk_t, "wvt": wv_t,
            "bqt": bqt, "bkt": bkt, "bvr": bvr,
            "msk": msk, "gut": gu_tiles, "wpt": wp_tiles,
        })
        meta.append((b, q0lo, q0hi))
    return in_maps, meta


def kernel(**inputs):
    in_maps, meta = stage_inputs(**inputs)
    nc = build_nc()
    res = run_bass_kernel_spmd(nc, in_maps, core_ids=list(range(8)))
    out = np.empty((B, S, E), np.float32)
    for core in range(8):
        b, q0lo, q0hi = meta[core]
        o = res.results[core]["out"]
        out[b, q0lo:q0lo + 512, :] = o[:, 0:512].T
        out[b, q0hi:q0hi + 512, :] = o[:, 512:1024].T
    return out


# version 17 (final)
# speedup vs baseline: 1.9817x; 1.0164x over previous
import numpy as np
import ml_dtypes
import concourse.bass as bass
import concourse.mybir as mybir
from concourse.bass_utils import run_bass_kernel_spmd

F32 = mybir.dt.float32
BF16 = mybir.dt.bfloat16
I16 = mybir.dt.int16
AF = mybir.ActivationFunctionType
ALU = mybir.AluOpType
BF = ml_dtypes.bfloat16

B, S, E = 4, 2048, 2048
H, G, D = 16, 4, 128
NE = 16
HID = 8192
LN_EPS = 1e-5
RMS_EPS = 1.1920928955078125e-07
ISQ = float(1.0 / np.sqrt(128.0))

ENGS = ("tensor", "vector", "scalar", "gpsimd", "sync")
DEBUG = False
TRACE = None


class Em:

    def __init__(self, cnt):
        self.q = {e: [] for e in ENGS}
        self.cnt = cnt

    def do(self, eng, fn, inc=None):
        if TRACE is not None:
            import traceback
            tag = traceback.extract_stack(limit=3)[0].lineno
            TRACE.append(("do", eng, inc[0] if inc else None,
                          self.cnt[inc[0]] + inc[1] if inc else None, tag))
        if inc is not None:
            name, amt = inc
            if amt == 16 or eng in ("vector", "scalar", "gpsimd"):
                self.wait(eng, name)
            self.cnt[name] += amt
            self.q[eng].append((fn, name, amt))
            return self.cnt[name]
        self.q[eng].append((fn, None, 0))
        return None

    def wait(self, eng, name, val=None):
        v = self.cnt[name] if val is None else val
        if TRACE is not None:
            import traceback
            tag = traceback.extract_stack(limit=3)[0].lineno
            TRACE.append(("wait", eng, name, v, tag))
        if v <= 0:
            return
        self.q[eng].append((("wait", name, v), None, 0))

    def barrier(self):
        for e in ENGS:
            for s in list(self.cnt):
                self.wait(e, s)

    def replay(self, block, semh):
        def mk(eng_name):
            items = list(self.q[eng_name])

            def body(EO):
                for fn, sname, amt in items:
                    if isinstance(fn, tuple) and fn[0] == "wait":
                        EO.wait_ge(semh[fn[1]], fn[2])
                    else:
                        inst = fn(EO)
                        if sname is not None:
                            inst.then_inc(semh[sname], amt)
            return body

        block.tensor(mk("tensor"))
        block.vector(mk("vector"))
        block.scalar(mk("scalar"))
        block.gpsimd(mk("gpsimd"))
        block.sync(mk("sync"))


def build_nc():
    nc = bass.Bass("TRN2", target_bir_lowering=False)

    d_xq = nc.dram_tensor("xq", [E, 1024], F32, kind="ExternalInput").ap()
    d_xlo = nc.dram_tensor("xlo", [E, 1024], F32, kind="ExternalInput").ap()
    d_xhi = nc.dram_tensor("xhi", [E, 2048], F32, kind="ExternalInput").ap()
    d_wq = nc.dram_tensor("wqt", [H, 128, 2048], F32, kind="ExternalInput").ap()
    d_wk = nc.dram_tensor("wkt", [G, 128, 2048], F32, kind="ExternalInput").ap()
    d_wv = nc.dram_tensor("wvt", [G, 128, 2048], F32, kind="ExternalInput").ap()
    d_bqt = nc.dram_tensor("bqt", [128, H], F32, kind="ExternalInput").ap()
    d_bkt = nc.dram_tensor("bkt", [128, G], F32, kind="ExternalInput").ap()
    d_bvr = nc.dram_tensor("bvr", [128, G * 128], F32, kind="ExternalInput").ap()
    d_msk = nc.dram_tensor("msk", [6, 128, 512], BF16, kind="ExternalInput").ap()
    d_gu = nc.dram_tensor("gut", [64, 128, 4096], F32, kind="ExternalInput").ap()
    d_wp = nc.dram_tensor("wpt", [32, 128, 4096], F32, kind="ExternalInput").ap()
    d_out = nc.dram_tensor("out", [E, 1024], F32, kind="ExternalOutput").ap()
    d_xsp = nc.dram_tensor("xsp", [E, 1024], F32,
                           kind="ExternalOutput" if DEBUG else "Internal").ap()
    d_whg = nc.dram_tensor("whg", [64, 128, 4096], BF16).ap()
    d_whp = nc.dram_tensor("whp", [32, 128, 4096], BF16).ap()
    if DEBUG:
        d_dq = nc.dram_tensor("dbg_q", [128, H * 1024], BF16, kind="ExternalOutput").ap()
        d_dkl = nc.dram_tensor("dbg_klo", [128, G * 1024], BF16, kind="ExternalOutput").ap()
        d_dkh = nc.dram_tensor("dbg_khi", [128, G * 2048], BF16, kind="ExternalOutput").ap()
        d_dvl = nc.dram_tensor("dbg_vlo", [128, G * 1024], BF16, kind="ExternalOutput").ap()
        d_dvh = nc.dram_tensor("dbg_vhi", [128, G * 2048], BF16, kind="ExternalOutput").ap()
        d_dxn2 = nc.dram_tensor("dbg_xn2", [128, NE * 1024], BF16, kind="ExternalOutput").ap()
        d_dxnq = nc.dram_tensor("dbg_xnq", [128, NE * 512], BF16, kind="ExternalOutput").ap()

    cnt = {s: 0 for s in ("SL", "GL", "SS", "TE", "VE", "SE")}

    import contextlib
    with contextlib.ExitStack() as ST:
        semh = {s: ST.enter_context(nc.semaphore(s)) for s in cnt}

        t_ones = ST.enter_context(nc.sbuf_tensor("ones_s", [128, 256], BF16))
        t_xn2 = ST.enter_context(nc.sbuf_tensor("xn2_s", [128, NE * 1024], BF16))
        ps = ST.enter_context(nc.psum_tensor("pp", [128, 4096], F32))

        def PSB(b, w=512):
            return ps[:, b * 512: b * 512 + w]

        ones128 = t_ones[:, 0:128]
        ones64 = t_ones[:, 128:256]

        def MM(E_, out, lhsT, rhs, start, stop):
            return E_.matmul(out, lhsT, rhs, start=start, stop=stop,
                             skip_group_check=True)

        with contextlib.ExitStack() as SA:
            t_bqt = SA.enter_context(nc.sbuf_tensor("bqt_s", [128, H], F32))
            t_bkt = SA.enter_context(nc.sbuf_tensor("bkt_s", [128, G], F32))
            t_bvr = SA.enter_context(nc.sbuf_tensor("bvr_s", [128, G * 128], F32))
            t_msk = SA.enter_context(nc.sbuf_tensor("msk_s", [128, 6 * 512], BF16))
            t_qall = SA.enter_context(nc.sbuf_tensor("qall", [128, H * 1024], BF16))
            t_klo = SA.enter_context(nc.sbuf_tensor("klo", [128, G * 1024], BF16))
            t_vlo = SA.enter_context(nc.sbuf_tensor("vlo", [128, G * 1024], BF16))
            t_khi = SA.enter_context(nc.sbuf_tensor("khi", [128, G * 2048], BF16))
            t_vhi = SA.enter_context(nc.sbuf_tensor("vhi", [128, G * 2048], BF16))
            t_pt = SA.enter_context(nc.sbuf_tensor("pt", [128, 8 * 512], BF16))
            t_xnb = SA.enter_context(nc.sbuf_tensor("xnb", [128, NE * 512], BF16))
            t_xnq = SA.enter_context(nc.sbuf_tensor("xnq", [128, NE * 512], BF16))
            t_wkv = SA.enter_context(nc.sbuf_tensor("wkv", [128, 2 * 4096], BF16))
            t_xf = SA.enter_context(nc.sbuf_tensor("xf", [128, 2 * 512], F32))
            t_st = SA.enter_context(nc.sbuf_tensor("stt", [128, 2 * 512], F32))
            t_rstd = SA.enter_context(nc.sbuf_tensor("rstd", [128, 512], BF16))
            t_stm = SA.enter_context(nc.sbuf_tensor("stm", [128, 512], BF16))
            t_xc = SA.enter_context(nc.sbuf_tensor("xc", [128, 2 * 512], F32))
            t_oc = SA.enter_context(nc.sbuf_tensor("oc", [128, 2 * 512], F32))
            t_wq2 = SA.enter_context(nc.sbuf_tensor("wq2", [128, 2 * 2048], BF16))
            t_cst = SA.enter_context(nc.sbuf_tensor("cst", [128, 4 * 512], BF16))

            em = Em(cnt)

            em.do("sync", lambda E_: E_.dma_start(out=t_bqt[:, :], in_=d_bqt), inc=("SL", 16))
            em.do("sync", lambda E_: E_.dma_start(out=t_bkt[:, :], in_=d_bkt), inc=("SL", 16))
            em.do("sync", lambda E_: E_.dma_start(out=t_bvr[:, :], in_=d_bvr), inc=("SL", 16))
            for mi in range(6):
                em.do("sync", lambda E_, mi=mi: E_.dma_start(
                    out=t_msk[:, mi * 512:(mi + 1) * 512], in_=d_msk[mi]), inc=("SL", 16))
            em.do("vector", lambda E_: E_.memset(ones128, 1.0), inc=("VE", 1))
            em.do("vector", lambda E_: E_.memset(ones64, 0.0), inc=("VE", 1))
            em.wait("vector", "VE")
            em.do("vector", lambda E_: E_.memset(t_ones[0:64, 128:192], 1.0), inc=("VE", 1))
            em.wait("vector", "VE")
            em.do("vector", lambda E_: E_.memset(t_ones[64:128, 192:256], 1.0), inc=("VE", 1))

            def MSK(i):
                return t_msk[:, i * 512:(i + 1) * 512]

            ln_prev = [0, 0]
            ln_prev_ve = [0, 0]
            ln_chain = [0]

            def ln_emit(src_fn, dst_fn):
                te_marks = {}
                cast_ve = {}
                sq_se = {}
                for e in range(NE):
                    xf = t_xf[:, (e % 2) * 512:(e % 2) * 512 + 512]
                    em.wait("sync", "VE", cast_ve[e - 2] if e >= 2 else ln_prev_ve[e])
                    em.do("sync", lambda E_, e=e, xf=xf: E_.dma_start(
                        out=xf, in_=src_fn(e)), inc=("SL", 16))
                    lv = cnt["SL"]
                    cast = dst_fn(e)
                    sq = t_cst[:, 1024 + (e % 2) * 512: 1024 + (e % 2) * 512 + 512]
                    guard = te_marks[e - 2] if e >= 2 else ln_prev[e]
                    em.wait("vector", "SL", lv)
                    if e == 0:
                        em.wait("vector", "TE")
                    em.do("vector", lambda E_, xf=xf, cast=cast: E_.tensor_copy(cast, xf),
                          inc=("VE", 1))
                    cast_ve[e] = cnt["VE"]
                    em.wait("scalar", "VE", cast_ve[e])
                    if guard:
                        em.wait("scalar", "TE", guard)
                    em.do("scalar", lambda E_, cast=cast, sq=sq: E_.activation(sq, cast, AF.Square),
                          inc=("SE", 1))
                    sq_se[e] = cnt["SE"]
                    em.wait("tensor", "VE", cast_ve[e])
                    em.wait("tensor", "SE", sq_se[e])
                    if e == 0:
                        em.wait("tensor", "VE", ln_chain[0])
                    em.do("tensor", lambda E_, e=e, cast=cast: MM(
                        E_, PSB(5), ones128, cast, e == 0, e == NE - 1), inc=("TE", 1))
                    te_marks[e] = em.do("tensor", lambda E_, e=e, sq=sq: MM(
                        E_, PSB(6), ones128, sq, e == 0, e == NE - 1), inc=("TE", 1))
                st_mean = t_stm[:, :]
                st_wk = t_st[:, 512:1024]
                em.wait("vector", "TE")
                em.do("vector", lambda E_: E_.tensor_scalar(
                    st_mean, PSB(5), 1.0 / E, None, ALU.mult), inc=("VE", 1))
                em.do("vector", lambda E_: E_.tensor_tensor(
                    st_wk, st_mean, st_mean, ALU.mult), inc=("VE", 1))
                em.do("vector", lambda E_: E_.scalar_tensor_tensor(
                    st_wk, PSB(6), 1.0 / E, st_wk, ALU.mult, ALU.subtract), inc=("VE", 1))
                ln_chain[0] = cnt["VE"]
                em.do("vector", lambda E_: E_.tensor_scalar(
                    st_wk, st_wk, LN_EPS, None, ALU.add), inc=("VE", 1))
                em.do("vector", lambda E_: E_.reciprocal(st_wk, st_wk), inc=("VE", 1))
                em.wait("scalar", "VE")
                em.do("scalar", lambda E_: E_.activation(t_rstd[:, :], st_wk, AF.Sqrt),
                      inc=("SE", 1))
                se_rstd = cnt["SE"]
                sub_ve = {}
                for e in range(NE):
                    dst = dst_fn(e)
                    tmp = t_cst[:, (e % 2) * 512:(e % 2) * 512 + 512]
                    if e == 0:
                        em.wait("vector", "SE", se_rstd)
                    em.do("vector", lambda E_, dst=dst, tmp=tmp: E_.tensor_tensor(
                        tmp, dst, st_mean, ALU.subtract), inc=("VE", 1))
                    em.do("vector", lambda E_, tmp=tmp, dst=dst: E_.tensor_tensor(
                        dst, tmp, t_rstd[:, :], ALU.mult), inc=("VE", 1))
                    sub_ve[e] = cnt["VE"]
                ln_prev[0] = te_marks[NE - 2]
                ln_prev[1] = te_marks[NE - 1]
                ln_prev_ve[0] = cast_ve[NE - 2]
                ln_prev_ve[1] = cast_ve[NE - 1]
                return cnt["VE"]

            ln_emit(lambda e: d_xq[e * 128:(e + 1) * 128, 0:512],
                    lambda e: t_xnq[:, e * 512:(e + 1) * 512])
            ln_emit(lambda e: d_xq[e * 128:(e + 1) * 128, 512:1024],
                    lambda e: t_xnb[:, e * 512:(e + 1) * 512])
            qte = {}
            for h in range(H):
                wbuf = t_wq2[:, (h % 2) * 2048:(h % 2) * 2048 + 2048]
                prev = qte.get(h - 2)
                if prev is not None:
                    em.wait("gpsimd", "TE", prev)
                em.do("gpsimd", lambda E_, h=h, wbuf=wbuf: E_.dma_start(
                    out=wbuf, in_=d_wq[h]), inc=("GL", 16))
                gl = cnt["GL"]
                em.wait("tensor", "GL", gl)
                em.wait("tensor", "VE")
                for hf in range(2):
                    xn = t_xnq if hf == 0 else t_xnb
                    for e in range(NE):
                        mark = em.do("tensor", lambda E_, e=e, wbuf=wbuf, xn=xn, hf=hf: MM(
                            E_, ps[:, 512 + hf * 512: 1024 + hf * 512],
                            wbuf[:, e * 128:(e + 1) * 128],
                            xn[:, e * 512:(e + 1) * 512], e == 0, e == NE - 1),
                            inc=("TE", 1) if (e == NE - 1 and hf == 1) else None)
                qte[h] = mark
                em.wait("vector", "TE", mark)
                em.do("vector", lambda E_, h=h: E_.tensor_scalar(
                    t_qall[:, h * 1024: h * 1024 + 1024],
                    ps[:, 512:1536], t_bqt[:, h:h + 1], None, ALU.add), inc=("VE", 1))

            kv_state = {"te": [0, 0], "ev": 0}

            def kv_emit(nblk, xsrc, kdst, vdst, seq0):
                for kb in range(nblk):
                    ln_ve = ln_emit(
                        lambda e, kb=kb: xsrc[e * 128:(e + 1) * 128, kb * 512:kb * 512 + 512],
                        lambda e: t_xnb[:, e * 512:(e + 1) * 512])
                    for g in range(G):
                        seq = seq0 + kb * G + g
                        par = seq % 2
                        wkb = t_wkv[:, par * 4096: par * 4096 + 2048]
                        wvb = t_wkv[:, par * 4096 + 2048: par * 4096 + 4096]
                        em.wait("gpsimd", "TE", kv_state["te"][par])
                        em.do("gpsimd", lambda E_, g=g, wkb=wkb: E_.dma_start(
                            out=wkb, in_=d_wk[g]), inc=("GL", 16))
                        em.do("gpsimd", lambda E_, g=g, wvb=wvb: E_.dma_start(
                            out=wvb, in_=d_wv[g]), inc=("GL", 16))
                        gl = cnt["GL"]
                        em.wait("tensor", "GL", gl)
                        em.wait("tensor", "VE", max(ln_ve, kv_state["ev"]))
                        for e in range(NE):
                            em.do("tensor", lambda E_, e=e, wkb=wkb: MM(
                                E_, PSB(0), wkb[:, e * 128:(e + 1) * 128],
                                t_xnb[:, e * 512:(e + 1) * 512], e == 0, e == NE - 1),
                                inc=("TE", 1) if e == NE - 1 else None)
                        for vc in range(4):
                            for e in range(NE):
                                em.do("tensor", lambda E_, e=e, vc=vc, wvb=wvb: MM(
                                    E_, ps[:, 1024 + vc * 128:1024 + vc * 128 + 128],
                                    t_xnb[:, e * 512 + vc * 128: e * 512 + vc * 128 + 128],
                                    wvb[:, e * 128:(e + 1) * 128], e == 0, e == NE - 1),
                                    inc=("TE", 1) if (e == NE - 1 and vc == 3) else None)
                        te = cnt["TE"]
                        kv_state["te"][par] = te
                        em.wait("vector", "TE", te)
                        em.do("vector", lambda E_, g=g, kb=kb: E_.tensor_scalar(
                            kdst(g)[:, kb * 512:(kb + 1) * 512], PSB(0),
                            t_bkt[:, g:g + 1], None, ALU.add), inc=("VE", 1))
                        for vc in range(4):
                            em.do("vector", lambda E_, g=g, kb=kb, vc=vc: E_.tensor_tensor(
                                vdst(g)[:, (kb * 4 + vc) * 128:(kb * 4 + vc) * 128 + 128],
                                ps[:, 1024 + vc * 128:1024 + vc * 128 + 128],
                                t_bvr[:, g * 128:(g + 1) * 128], ALU.add), inc=("VE", 1))
                        kv_state["ev"] = cnt["VE"]

            kv_emit(2, d_xlo, lambda g: t_klo[:, g * 1024:(g + 1) * 1024],
                    lambda g: t_vlo[:, g * 1024:(g + 1) * 1024], 0)
            kv_emit(4, d_xhi, lambda g: t_khi[:, g * 2048:(g + 1) * 2048],
                    lambda g: t_vhi[:, g * 2048:(g + 1) * 2048], 8)

            em.wait("tensor", "VE")
            ove_m = {0: 0, 1: 0}
            ss_m = {0: 0, 1: 0}
            for h in range(H):
                g = h // 4
                for qt in range(2):
                    nw = 8 if qt == 0 else 16
                    kt = t_klo[:, g * 1024:(g + 1) * 1024] if qt == 0 else t_khi[:, g * 2048:(g + 1) * 2048]
                    vt = t_vlo[:, g * 1024:(g + 1) * 1024] if qt == 0 else t_vhi[:, g * 2048:(g + 1) * 2048]
                    qrhs = t_qall[:, h * 1024 + qt * 512: h * 1024 + qt * 512 + 512]
                    npair = nw // 2
                    exp_se = {}
                    avte = {}
                    xcb = t_xc[:, qt * 512:qt * 512 + 512]
                    em.wait("sync", "VE", ove_m[qt])
                    em.do("sync", lambda E_, h=h, qt=qt, xcb=xcb: E_.dma_start(
                        out=xcb, in_=d_xq[h * 128:(h + 1) * 128, qt * 512:(qt + 1) * 512]),
                        inc=("SL", 16))
                    xl = cnt["SL"]
                    em.wait("tensor", "SE")
                    pw = {}

                    def emit_sumav(k):
                        em.wait("tensor", pw[k][0], pw[k][1])
                        for j in range(2):
                            w = 2 * k + j
                            em.do("tensor", lambda E_, w=w, nw=nw: MM(
                                E_, PSB(2), ones128,
                                t_pt[:, (w % 8) * 512:(w % 8) * 512 + 512],
                                w == 0, w == nw - 1))
                        for j in range(2):
                            w = 2 * k + j
                            em.do("tensor", lambda E_, w=w, nw=nw, vt=vt: MM(
                                E_, PSB(3), vt[:, w * 128:(w + 1) * 128],
                                t_pt[:, (w % 8) * 512:(w % 8) * 512 + 512],
                                w == 0, w == nw - 1), inc=("TE", 1) if j == 1 else None)
                        avte[k] = cnt["TE"]

                    for i in range(npair):
                        bsel = 4 + 2 * (i % 2)
                        if i >= 2:
                            em.wait("tensor", "SE", exp_se[i - 2])
                        for j in range(2):
                            w = 2 * i + j
                            em.do("tensor", lambda E_, w=w, bsel=bsel, j=j, kt=kt, qrhs=qrhs: MM(
                                E_, PSB(bsel + j), kt[:, w * 128:(w + 1) * 128], qrhs,
                                True, True), inc=("TE", 1) if j == 1 else None)
                        sc_te = cnt["TE"]
                        em.wait("scalar", "TE", sc_te)
                        if i >= 4:
                            em.wait("scalar", "TE", avte[i - 4])
                        em.do("scalar", lambda E_, i=i, bsel=bsel: E_.activation(
                            t_pt[:, (2 * i % 8) * 512: (2 * i % 8) * 512 + 1024],
                            ps[:, bsel * 512: bsel * 512 + 1024], AF.Exp, scale=ISQ),
                            inc=("SE", 1))
                        exp_se[i] = cnt["SE"]
                        masked = []
                        for j in range(2):
                            w = 2 * i + j
                            mi = None
                            if qt == 0:
                                mi = 0 if w < 4 else 2 + (w - 4)
                            else:
                                if w < 4:
                                    mi = 1
                                elif w >= 12:
                                    mi = 2 + (w - 12)
                            if mi is not None:
                                masked.append((w, mi))
                        if masked:
                            em.wait("vector", "SE", exp_se[i])
                            for w, mi in masked:
                                em.do("vector", lambda E_, w=w, mi=mi: E_.tensor_tensor(
                                    t_pt[:, (w % 8) * 512:(w % 8) * 512 + 512],
                                    t_pt[:, (w % 8) * 512:(w % 8) * 512 + 512],
                                    MSK(mi), ALU.mult), inc=("VE", 1))
                            pw[i] = ("VE", cnt["VE"])
                        else:
                            pw[i] = ("SE", exp_se[i])
                        if i >= 1:
                            emit_sumav(i - 1)
                    emit_sumav(npair - 1)
                    em.wait("vector", "TE", avte[npair - 1])
                    em.do("vector", lambda E_: E_.reciprocal(t_st[:, 0:512], PSB(2)),
                          inc=("VE", 1))
                    em.do("vector", lambda E_: E_.tensor_tensor(
                        t_st[:, 512:1024], PSB(3), t_st[:, 0:512], ALU.mult), inc=("VE", 1))
                    em.wait("vector", "SL", xl)
                    if ss_m[qt]:
                        em.wait("vector", "SS", ss_m[qt])
                    ocb = t_oc[:, qt * 512:qt * 512 + 512]
                    em.do("vector", lambda E_, xcb=xcb, ocb=ocb: E_.tensor_tensor(
                        ocb, t_st[:, 512:1024], xcb, ALU.add), inc=("VE", 1))
                    ove = cnt["VE"]
                    ove_m[qt] = ove
                    em.wait("sync", "VE", ove)
                    em.do("sync", lambda E_, h=h, qt=qt, ocb=ocb: E_.dma_start(
                        out=d_xsp[h * 128:(h + 1) * 128, qt * 512:(qt + 1) * 512], in_=ocb),
                        inc=("SS", 16))
                    ss_m[qt] = cnt["SS"]

            em.wait("sync", "SS")
            em.barrier()
            if DEBUG:
                em.do("sync", lambda E_: E_.dma_start(out=d_dq, in_=t_qall[:, :]), inc=("SS", 16))
                em.do("sync", lambda E_: E_.dma_start(out=d_dkl, in_=t_klo[:, :]), inc=("SS", 16))
                em.do("sync", lambda E_: E_.dma_start(out=d_dkh, in_=t_khi[:, :]), inc=("SS", 16))
                em.do("sync", lambda E_: E_.dma_start(out=d_dvl, in_=t_vlo[:, :]), inc=("SS", 16))
                em.do("sync", lambda E_: E_.dma_start(out=d_dvh, in_=t_vhi[:, :]), inc=("SS", 16))
            for hf in range(2):
                ln_emit(lambda e, hf=hf: d_xsp[e * 128:(e + 1) * 128, hf * 512:hf * 512 + 512],
                        lambda e, hf=hf: t_xn2[:, e * 1024 + hf * 512: e * 1024 + hf * 512 + 512])
            if DEBUG:
                em.barrier()
                em.do("sync", lambda E_: E_.dma_start(out=d_dxn2, in_=t_xn2[:, :]), inc=("SS", 16))
            em.barrier()

            with nc.Block() as blk:
                em.replay(blk, semh)

        with contextlib.ExitStack() as SF:
            t_h = SF.enter_context(nc.sbuf_tensor("hsb", [128, 64 * 512], BF16))
            t_wbf = SF.enter_context(nc.sbuf_tensor("wbf", [128, 3 * 4096], BF16))
            t_wh = SF.enter_context(nc.sbuf_tensor("wh", [128, 3 * 4096], BF16))
            t_qa = SF.enter_context(nc.sbuf_tensor("qa", [128, 2 * 4096], BF16))
            t_qb = SF.enter_context(nc.sbuf_tensor("qb", [128, 4096], BF16))
            t_qc = SF.enter_context(nc.sbuf_tensor("qc", [128, 2 * 4096], BF16))
            t_sg = SF.enter_context(nc.sbuf_tensor("sg", [128, 4 * 512], BF16))
            t_xo = SF.enter_context(nc.sbuf_tensor("xo", [128, 5 * 512], F32))
            t_fst = SF.enter_context(nc.sbuf_tensor("fst", [128, 2 * 512], F32))

            em = Em(cnt)
            em.barrier()

            tau_se = [0, 0]
            tau_rd = [0, 0]

            def QA(i):
                return t_qa[:, (i % 2) * 4096:(i % 2) * 4096 + 4096]

            def QC(i):
                return t_qc[:, (i % 2) * 4096:(i % 2) * 4096 + 4096]

            def quant_a(i, src_, gl_mark, tile_n):
                em.wait("vector", "GL", gl_mark)
                em.wait("vector", "SE", tau_rd_se_guard(i))
                em.do("vector", lambda E_, src_=src_, i=i: E_.tensor_scalar(
                    QA(i).bitcast(I16), src_.bitcast(I16), 0x7FFF, None,
                    ALU.bitwise_and), inc=("VE", 1))
                ve_abs = cnt["VE"]
                em.wait("scalar", "VE", tau_rd[i % 2])
                per = (4096 // tile_n) // 8
                for rnd in range(8):
                    bank = rnd % 2
                    em.wait("tensor", "VE", ve_abs)
                    em.wait("tensor", "SE", tau_se[bank])
                    for it in range(per):
                        c0 = (rnd * per + it) * tile_n
                        em.do("tensor", lambda E_, c0=c0, it=it, tile_n=tile_n, bank=bank, i=i: MM(
                            E_, ps[:, bank * 512 + it * tile_n: bank * 512 + it * tile_n + tile_n],
                            ones64, QA(i)[:, c0:c0 + tile_n], True, True),
                            inc=("TE", 1) if it == per - 1 else None)
                    te = cnt["TE"]
                    em.wait("scalar", "TE", te)
                    em.do("scalar", lambda E_, rnd=rnd, bank=bank, i=i: E_.mul(
                        QC(i)[:, rnd * 512:(rnd + 1) * 512],
                        ps[:, bank * 512: bank * 512 + 512], 1.0 / 128),
                        inc=("SE", 1))
                    tau_se[bank] = cnt["SE"]
                return cnt["SE"]

            def tau_rd_se_guard(i):
                return 0

            def quant_b(i, src_, dst, tauev, te_guard, ss_guard):
                em.wait("vector", "SE", tauev)
                em.do("vector", lambda E_, i=i: E_.tensor_tensor(
                    t_qb[:, :], QA(i), QC(i), ALU.is_gt), inc=("VE", 1))
                em.do("vector", lambda E_, i=i: E_.tensor_tensor(
                    t_qb[:, :], t_qb[:, :], QC(i), ALU.mult), inc=("VE", 1))
                em.do("vector", lambda E_: E_.tensor_scalar(
                    t_qb[:, :], t_qb[:, :], 2.0, None, ALU.mult), inc=("VE", 1))
                em.do("vector", lambda E_, src_=src_, i=i: E_.tensor_scalar(
                    QA(i).bitcast(I16), src_.bitcast(I16), -0x8000, None,
                    ALU.bitwise_and), inc=("VE", 1))
                if te_guard:
                    em.wait("vector", "TE", te_guard)
                if ss_guard:
                    em.wait("vector", "SS", ss_guard)
                em.do("vector", lambda E_, dst=dst, i=i: E_.tensor_tensor(
                    dst.bitcast(I16), t_qb[:, :].bitcast(I16), QA(i).bitcast(I16),
                    ALU.bitwise_or), inc=("VE", 1))
                tau_rd[i % 2] = cnt["VE"]
                return cnt["VE"]

            for p in range(2):
                gate_te = {}
                gu_te = {}
                q_ve = {}
                tauev = {}
                st_ss = {}
                h_ve = {}
                silu_se = {}
                sumsq_te = {}
                wh_sl = {}

                def WB(i):
                    return t_wbf[:, (i % 3) * 4096:(i % 3) * 4096 + 4096]

                def WH(i):
                    return t_wh[:, (i % 3) * 4096:(i % 3) * 4096 + 4096]

                for i in range(67):
                    if i < 64:
                        if p == 0:
                            if i >= 3:
                                em.wait("gpsimd", "VE", q_ve[i - 3])
                            em.do("gpsimd", lambda E_, i=i: E_.dma_start(
                                out=WB(i), in_=d_gu[i]), inc=("GL", 16))
                            tauev[i] = quant_a(i, WB(i), cnt["GL"], 256)
                        else:
                            if i >= 3:
                                em.wait("sync", "TE", gu_te[i - 3])
                            em.do("sync", lambda E_, i=i: E_.dma_start(
                                out=WH(i), in_=d_whg[i]), inc=("SL", 16))
                            wh_sl[i] = cnt["SL"]
                    if p == 0 and 1 <= i <= 64:
                        j = i - 1
                        q_ve[j] = quant_b(j, WB(j), WH(j), tauev[j],
                                          gu_te.get(j - 3, 0), st_ss.get(j - 3, 0))
                        em.wait("sync", "VE", q_ve[j])
                        em.do("sync", lambda E_, j=j: E_.dma_start(
                            out=d_whg[j], in_=WH(j)), inc=("SS", 16))
                        st_ss[j] = cnt["SS"]
                    if 2 <= i <= 65:
                        j = i - 2
                        wh = WH(j)
                        gbank = 4 + 3 * (j % 2)
                        ubank = 5 - 3 * (j % 2)
                        if p == 0:
                            em.wait("tensor", "VE", q_ve[j])
                        else:
                            em.wait("tensor", "SL", wh_sl[j])
                        if j >= 2:
                            em.wait("tensor", "SE", silu_se[j - 2])
                        for kc in range(NE):
                            em.do("tensor", lambda E_, kc=kc, wh=wh, p=p, gbank=gbank: MM(
                                E_, PSB(gbank), wh[:, kc * 256: kc * 256 + 128],
                                t_xn2[:, kc * 1024 + p * 512: kc * 1024 + p * 512 + 512],
                                kc == 0, kc == NE - 1),
                                inc=("TE", 1) if kc == NE - 1 else None)
                        gate_te[j] = cnt["TE"]
                        if j >= 2:
                            em.wait("tensor", "VE", h_ve[j - 2])
                        for kc in range(NE):
                            em.do("tensor", lambda E_, kc=kc, wh=wh, p=p, ubank=ubank: MM(
                                E_, PSB(ubank), wh[:, kc * 256 + 128: kc * 256 + 256],
                                t_xn2[:, kc * 1024 + p * 512: kc * 1024 + p * 512 + 512],
                                kc == 0, kc == NE - 1),
                                inc=("TE", 1) if kc == NE - 1 else None)
                        gu_te[j] = cnt["TE"]
                        sg = t_sg[:, (j % 2) * 512:(j % 2) * 512 + 512]
                        h2 = t_sg[:, 1024 + (j % 2) * 512: 1024 + (j % 2) * 512 + 512]
                        em.wait("scalar", "TE", gate_te[j])
                        if j >= 2:
                            em.wait("scalar", "VE", h_ve[j - 2])
                        em.do("scalar", lambda E_, gbank=gbank, sg=sg: E_.activation(
                            sg, PSB(gbank), AF.Silu), inc=("SE", 1))
                        silu_se[j] = cnt["SE"]
                        em.wait("vector", "SE", silu_se[j])
                        em.wait("vector", "TE", gu_te[j])
                        em.do("vector", lambda E_, j=j, sg=sg, ubank=ubank: E_.tensor_tensor(
                            t_h[:, j * 512:(j + 1) * 512], PSB(ubank), sg, ALU.mult),
                            inc=("VE", 1))
                        if j >= 2:
                            em.wait("vector", "TE", sumsq_te[j - 2])
                        em.do("vector", lambda E_, j=j, h2=h2: E_.tensor_tensor(
                            h2, t_h[:, j * 512:(j + 1) * 512],
                            t_h[:, j * 512:(j + 1) * 512], ALU.mult), inc=("VE", 1))
                        h_ve[j] = cnt["VE"]
                    if 3 <= i <= 66:
                        j = i - 3
                        h2 = t_sg[:, 1024 + (j % 2) * 512: 1024 + (j % 2) * 512 + 512]
                        em.wait("tensor", "VE", h_ve[j])
                        em.do("tensor", lambda E_, j=j, h2=h2: MM(
                            E_, PSB(6), ones128, h2, j == 0, j == 63), inc=("TE", 1))
                        sumsq_te[j] = cnt["TE"]
                em.wait("vector", "TE", sumsq_te[63])
                em.do("vector", lambda E_: E_.tensor_scalar(
                    t_fst[:, 0:512], PSB(6), 1.0 / HID, RMS_EPS, ALU.mult, ALU.add),
                    inc=("VE", 1))
                em.do("vector", lambda E_: E_.reciprocal(t_fst[:, 0:512], t_fst[:, 0:512]),
                      inc=("VE", 1))
                em.wait("scalar", "VE")
                em.do("scalar", lambda E_: E_.activation(
                    t_fst[:, 512:1024], t_fst[:, 0:512], AF.Sqrt), inc=("SE", 1))
                rsq_se = cnt["SE"]
                pj_te = {-3: gu_te[63], -2: gu_te[61], -1: gu_te[62]}
                pj_ve = {-3: q_ve.get(63, 0), -2: q_ve.get(61, 0), -1: q_ve.get(62, 0)}
                pj_ss = {-3: st_ss.get(63, 0), -2: st_ss.get(61, 0), -1: st_ss.get(62, 0)}
                pj_tauev = {}
                pj_sl = {}
                ev_ve = {}
                out_ss = {}
                hve_all = h_ve[63]
                for ii in range(34):
                    if ii < 32:
                        idx = ii
                        if (ii % 2) == 0:
                            dc = ii // 2
                            xcb = t_xo[:, (dc % 2) * 512:(dc % 2) * 512 + 512]
                            if dc >= 2:
                                em.wait("sync", "VE", ev_ve[dc - 2])
                            em.do("sync", lambda E_, dc=dc, p=p, xcb=xcb: E_.dma_start(
                                out=xcb,
                                in_=d_xsp[dc * 128:(dc + 1) * 128, p * 512:(p + 1) * 512]),
                                inc=("SL", 16))
                            pj_sl[("x", dc)] = cnt["SL"]
                        if p == 0:
                            em.wait("gpsimd", "VE", pj_ve[idx - 3])
                            em.do("gpsimd", lambda E_, idx=idx: E_.dma_start(
                                out=WB(idx), in_=d_wp[idx]), inc=("GL", 16))
                            pj_tauev[idx] = quant_a(idx, WB(idx), cnt["GL"], 128)
                        else:
                            em.wait("sync", "TE", pj_te[idx - 3])
                            em.do("sync", lambda E_, idx=idx: E_.dma_start(
                                out=WH(idx), in_=d_whp[idx]), inc=("SL", 16))
                            pj_sl[idx] = cnt["SL"]
                    if p == 0 and 1 <= ii <= 32:
                        idx = ii - 1
                        pj_ve[idx] = quant_b(idx, WB(idx), WH(idx), pj_tauev[idx],
                                             pj_te[idx - 3], pj_ss[idx - 3])
                        em.wait("sync", "VE", pj_ve[idx])
                        em.do("sync", lambda E_, idx=idx: E_.dma_start(
                            out=d_whp[idx], in_=WH(idx)), inc=("SS", 16))
                        pj_ss[idx] = cnt["SS"]
                    if 2 <= ii <= 33:
                        idx = ii - 2
                        dc, half = idx // 2, idx % 2
                        pbank = 7 - 4 * (dc % 2)
                        wh = WH(idx)
                        if p == 0:
                            em.wait("tensor", "VE", pj_ve[idx])
                        else:
                            em.wait("tensor", "SL", pj_sl[idx])
                        if idx == 0:
                            em.wait("tensor", "VE", hve_all)
                        if half == 0 and dc >= 2:
                            em.wait("tensor", "VE", ev_ve[dc - 2])
                        for hc in range(32):
                            hcg = half * 32 + hc
                            em.do("tensor", lambda E_, hc=hc, hcg=hcg, wh=wh, pbank=pbank: MM(
                                E_, PSB(pbank), wh[:, hc * 128:(hc + 1) * 128],
                                t_h[:, hcg * 512:(hcg + 1) * 512], hcg == 0, hcg == 63),
                                inc=("TE", 1) if hc == 31 else None)
                        pj_te[idx] = cnt["TE"]
                        if half == 1:
                            em.wait("vector", "TE", pj_te[idx])
                            em.wait("vector", "SE", rsq_se)
                            ob = t_xo[:, (3 + dc % 2) * 512:(3 + dc % 2) * 512 + 512]
                            em.do("vector", lambda E_, pbank=pbank: E_.tensor_tensor(
                                t_xo[:, 1024:1536], PSB(pbank), t_fst[:, 512:1024],
                                ALU.mult), inc=("VE", 1))
                            em.wait("vector", "SL", pj_sl[("x", dc)])
                            if dc >= 2:
                                em.wait("vector", "SS", out_ss[dc - 2])
                            xcb = t_xo[:, (dc % 2) * 512:(dc % 2) * 512 + 512]
                            em.do("vector", lambda E_, xcb=xcb, ob=ob: E_.tensor_tensor(
                                ob, t_xo[:, 1024:1536], xcb, ALU.add), inc=("VE", 1))
                            ev_ve[dc] = cnt["VE"]
                            em.wait("sync", "VE", ev_ve[dc])
                            em.do("sync", lambda E_, dc=dc, p=p, ob=ob: E_.dma_start(
                                out=d_out[dc * 128:(dc + 1) * 128, p * 512:(p + 1) * 512],
                                in_=ob), inc=("SS", 16))
                            out_ss[dc] = cnt["SS"]
                em.barrier()

            em.wait("sync", "SS")

            with nc.Block() as blk:
                em.replay(blk, semh)

    return nc


def stage_inputs(L0_attn_x, L0_attn_Wq, L0_attn_bq, L0_attn_Wk, L0_attn_bk,
                 L0_attn_Wv, L0_attn_bv, L0_ffn_w_gate_up, L0_ffn_w_proj):
    x = np.asarray(L0_attn_x, dtype=np.float32)
    xT = np.ascontiguousarray(x.transpose(0, 2, 1))

    wq_t = np.ascontiguousarray(
        np.asarray(L0_attn_Wq, np.float32).reshape(H, NE, 128, 128).transpose(0, 2, 1, 3)
        .reshape(H, 128, NE * 128))
    wk_t = np.ascontiguousarray(
        np.asarray(L0_attn_Wk, np.float32).reshape(G, NE, 128, 128).transpose(0, 2, 1, 3)
        .reshape(G, 128, NE * 128))
    wv_t = np.ascontiguousarray(
        np.asarray(L0_attn_Wv, np.float32).reshape(G, NE, 128, 128).transpose(0, 2, 1, 3)
        .reshape(G, 128, NE * 128))
    bqt = np.ascontiguousarray(np.asarray(L0_attn_bq, np.float32).T)
    bkt = np.ascontiguousarray(np.asarray(L0_attn_bk, np.float32).T)
    bvr = np.ascontiguousarray(
        np.broadcast_to(np.asarray(L0_attn_bv, np.float32)[:, None, :], (G, 128, 128))
        .transpose(1, 0, 2).reshape(128, G * 128))

    wgu = np.asarray(L0_ffn_w_gate_up, np.float32)
    wt3 = np.ascontiguousarray(wgu.T).reshape(NE, 128, 2 * HID)
    gate = wt3[:, :, :HID].reshape(NE, 128, 64, 128)
    up = wt3[:, :, HID:].reshape(NE, 128, 64, 128)
    stk = np.stack([gate, up], axis=3)
    gu_tiles = np.ascontiguousarray(stk.transpose(2, 1, 0, 3, 4).reshape(64, 128, 4096))

    wp = np.asarray(L0_ffn_w_proj, np.float32)
    arr = np.ascontiguousarray(wp.T).reshape(2, 32, 128, 16, 128)
    wp_tiles = np.ascontiguousarray(
        arr.transpose(3, 0, 2, 1, 4).reshape(32, 128, 4096))

    kk = np.arange(128)[:, None]
    qq = np.arange(512)[None, :]
    diag = [(kk + 128 * i <= qq).astype(BF) for i in range(4)]

    in_maps = []
    meta = []
    for core in range(8):
        b, p = core // 2, core % 2
        q0lo = 512 * p
        q0hi = 1536 - 512 * p
        own = np.concatenate([xT[b][:, q0lo:q0lo + 512], xT[b][:, q0hi:q0hi + 512]], axis=1)
        xlo = np.zeros((E, 1024), np.float32)
        lo0 = q0lo - 512
        s0, s1 = max(0, lo0), min(S, lo0 + 1024)
        xlo[:, s0 - lo0:s1 - lo0] = xT[b][:, s0:s1]
        xhi = np.zeros((E, 2048), np.float32)
        hi0 = q0hi - 1536
        s0, s1 = max(0, hi0), min(S, hi0 + 2048)
        xhi[:, s0 - hi0:s1 - hi0] = xT[b][:, s0:s1]
        prelo = np.full((128, 512), 0.0 if p == 0 else 1.0, BF)
        prehi = np.full((128, 512), 1.0 if p == 0 else 0.0, BF)
        msk = np.ascontiguousarray(np.stack([prelo, prehi] + diag))
        in_maps.append({
            "xq": np.ascontiguousarray(own),
            "xlo": xlo, "xhi": xhi,
            "wqt": wq_t, "wkt": wk_t, "wvt": wv_t,
            "bqt": bqt, "bkt": bkt, "bvr": bvr,
            "msk": msk, "gut": gu_tiles, "wpt": wp_tiles,
        })
        meta.append((b, q0lo, q0hi))
    return in_maps, meta


def kernel(**inputs):
    in_maps, meta = stage_inputs(**inputs)
    nc = build_nc()
    res = run_bass_kernel_spmd(nc, in_maps, core_ids=list(range(8)))
    out = np.empty((B, S, E), np.float32)
    for core in range(8):
        b, q0lo, q0hi = meta[core]
        o = res.results[core]["out"]
        out[b, q0lo:q0lo + 512, :] = o[:, 0:512].T
        out[b, q0hi:q0hi + 512, :] = o[:, 512:1024].T
    return out
